# version 22
# baseline (speedup 1.0000x reference)
import numpy as np
import ml_dtypes
from contextlib import ExitStack
from types import SimpleNamespace

import concourse.bass as bass
import concourse.mybir as mybir
from concourse.bass_utils import run_bass_kernel_spmd

F32 = mybir.dt.float32
BF16 = mybir.dt.bfloat16
I32 = mybir.dt.int32
AF = mybir.ActivationFunctionType
ALU = mybir.AluOpType
AX = mybir.AxisListType
BF = ml_dtypes.bfloat16

DBG = dict(const_c=False, no_lsum=False, norm_tail_const=False,
           norm_no_reduce=False, norm_no_square=False,
           norm_no_rounds=False)

H, DN, DV, DQ = 32, 128, 128, 192
D = 4096
S = 2048
B = 4
HPC = 16
NOWN = 1024
NKC = D // 128
SCALE = float(np.float32(DQ) ** -0.5)
NEG = -1.0e30
EPS = 1e-5
GROUPS = [[0, 1], [2, 3], [4, 5], [6, 7]]


class Sem:
    __slots__ = ("name", "n", "h")

    def __init__(self, name):
        self.name, self.n, self.h = name, 0, None


class Sched:
    ENGS = ("t", "v", "s", "q", "y")

    def __init__(self):
        self.ops = {k: [] for k in self.ENGS}
        self.sems = []
        self.chain = {}

    def sem(self, name):
        s = Sem(name)
        self.sems.append(s)
        return s

    def op(self, eng, fn, waits=(), incs=(), chainable=True):
        ws = []
        for w in waits:
            if isinstance(w, tuple):
                ws.append((w[0], w[1]))
            else:
                ws.append((w, w.n))
        cs = self.chain.get(eng) if chainable else None
        if cs is not None and cs.n > 0:
            ws.append((cs, cs.n))
        ics = []
        for s, k in incs:
            s.n += k
            ics.append((s, k))
        if cs is not None and not any(s is cs for s, _ in ics):
            cs.n += 1
            ics.append((cs, 1))
        self.ops[eng].append((fn, ws, ics))

    def emit(self, nc, ctx):
        for s in self.sems:
            s.h = ctx.enter_context(nc.semaphore(s.name))
        block = ctx.enter_context(nc.Block())
        deco = dict(t=block.tensor, v=block.vector, s=block.scalar,
                    q=block.gpsimd, y=block.sync)

        def runner(ops):
            def run(eng):
                waited = {}
                for fn, ws, ics in ops:
                    for s, v in ws:
                        if v > 0 and waited.get(s, 0) < v:
                            eng.wait_ge(s.h, v)
                            waited[s] = v
                    inst = fn(eng)
                    for s, k in ics:
                        inst = inst.then_inc(s.h, k)
            return run
        for key in self.ENGS:
            deco[key](runner(list(self.ops[key])))


class Lanes:

    def __init__(self, sch, name, n):
        self.sch = sch
        self.sems = [sch.sem(f"{name}{i}") for i in range(n)]
        self.n = n
        self.i = 0

    def dma(self, eng, fn, waits=()):
        s = self.sems[self.i % self.n]
        self.i += 1
        w = list(waits)
        if s.n > 0:
            w.append((s, s.n))
        self.sch.op(eng, fn, waits=w, incs=[(s, 16)], chainable=False)
        return (s, s.n)

    def totals(self):
        return [(s, s.n) for s in self.sems if s.n > 0]


def call(name, *args, **kw):
    return lambda e, n=name, a=args, k=kw: getattr(e, n)(*a, **k)


def _phaseA(E, actx):
    sch, nc = E.sch, E.nc
    sT, sV, sS, sQ = E.sT, E.sV, E.sS, E.sQ
    yL, qL, oL = E.yL, E.qL, E.oL
    ident, ones, onesrow, mask, maskT = E.ident, E.ones, E.onesrow, E.mask, E.maskT
    negrow = E.negrow
    hsT, wqkv = E.hsT, E.wqkv
    consts = E.consts

    xb = actx.enter_context(nc.sbuf_tensor("xb", [128, NKC * S], BF16))
    wq_sb = actx.enter_context(nc.sbuf_tensor("wq_sb", [128, NKC * 128], BF16))
    wk_sb = actx.enter_context(nc.sbuf_tensor("wk_sb", [128, NKC * 128], BF16))
    wv_sb = actx.enter_context(nc.sbuf_tensor("wv_sb", [128, NKC * 128], BF16))
    wcol = (wq_sb, wk_sb, wv_sb)
    cdma = {}

    def dma_col(h, col, waits=()):
        cdma[(h, col)] = qL.dma("q", call(
            "dma_start", out=wcol[col][:],
            in_=wqkv[h * 128:(h + 1) * 128, col * (NKC * 128):(col + 1) * (NKC * 128)]),
            waits=waits)

    for c in range(3):
        dma_col(0, c)

    with (nc.sbuf_tensor("xstage", [128, 2 * S], F32) as xstage,
          nc.sbuf_tensor("xsq", [128, 2 * S], BF16) as xsq,
          nc.sbuf_tensor("m_sb", [128, S], F32) as m_sb,
          nc.sbuf_tensor("e2_sb", [128, S], F32) as e2_sb,
          nc.sbuf_tensor("r_sb", [128, S], F32) as r_sb,
          nc.psum_tensor("psSum", [128, S], F32) as psSum,
          nc.psum_tensor("psSq", [128, S], F32) as psSq):
        vdone, sdone, tdone = {}, {}, {}
        slots = [xstage[:, 0:S], xstage[:, S:2 * S], r_sb[:]]
        for kc in range(NKC):
            pg = kc % 2
            stg = slots[kc % 3]
            waits = []
            if kc >= 3:
                waits = [(sV, vdone[kc - 3]), (sS, sdone[kc - 3])]
            xeng = "y" if kc % 2 == 0 else "s"
            xlane = yL if kc % 2 == 0 else E.sL
            tok = xlane.dma(xeng, call("dma_start", out=stg,
                                       in_=hsT[kc * 128:(kc + 1) * 128, :]), waits=waits)
            sch.op("v", call("tensor_copy", xb[:, kc * S:(kc + 1) * S], stg),
                   waits=[tok], incs=[(sV, 1)])
            vdone[kc] = sV.n
            swaits = [tok] if kc < 2 else [tok, (sT, tdone[kc - 2])]
            sch.op("s", call("activation", xsq[:, pg * S:(pg + 1) * S],
                             stg, AF.Square),
                   waits=swaits, incs=[(sS, 1)])
            sdone[kc] = sS.n
            for tc in range(4):
                sch.op("t", call("matmul", psSum[:, tc * 512:(tc + 1) * 512],
                                 ones[:], xb[:, kc * S + tc * 512: kc * S + (tc + 1) * 512],
                                 start=(kc == 0), stop=(kc == NKC - 1)),
                       waits=[sV, sS, consts] if tc == 0 else [])
            for tc in range(4):
                sch.op("t", call("matmul", psSq[:, tc * 512:(tc + 1) * 512],
                                 ones[:], xsq[:, pg * S + tc * 512: pg * S + (tc + 1) * 512],
                                 start=(kc == 0), stop=(kc == NKC - 1)),
                       incs=[(sT, 1)] if tc == 3 else [])
            tdone[kc] = sT.n
        sch.op("v", call("tensor_scalar", m_sb[:], psSum[:], 1.0 / D, None, ALU.mult),
               waits=[(sT, sT.n)], incs=[(sV, 1)])
        sch.op("v", call("tensor_scalar", e2_sb[:], psSq[:], 1.0 / D, None, ALU.mult),
               incs=[(sV, 1)])
        sch.op("s", call("activation", r_sb[:], m_sb[:], AF.Square),
               waits=[sV], incs=[(sS, 1)])
        sch.op("v", call("tensor_tensor", e2_sb[:], e2_sb[:], r_sb[:], ALU.subtract),
               waits=[sS])
        sch.op("v", call("tensor_scalar", e2_sb[:], e2_sb[:], EPS, None, ALU.add),
               incs=[(sV, 1)])
        sch.op("s", call("activation", r_sb[:], e2_sb[:], AF.Ln),
               waits=[sV], incs=[(sS, 1)])
        sch.op("s", call("activation", r_sb[:], r_sb[:], AF.Exp, scale=-0.5),
               incs=[(sS, 1)])
        m16 = xsq[:, 0:S]
        r16 = xsq[:, S:2 * S]
        sch.op("v", call("tensor_copy", m16, m_sb[:]), waits=[(sT, tdone[NKC - 1])])
        sch.op("v", call("tensor_copy", r16, r_sb[:]), waits=[sS], incs=[(sV, 1)])
        mr_tok = sV.n
        vnormk = {}
        for kc in range(NKC):
            eng = "v" if kc % 2 == 0 else "q"
            w0 = [] if eng == "v" else [(sV, mr_tok), (sT, tdone[kc])]
            sch.op(eng, call("tensor_tensor", xb[:, kc * S:(kc + 1) * S],
                             xb[:, kc * S:(kc + 1) * S], m16, ALU.subtract),
                   waits=w0)
            sch.op(eng, call("tensor_tensor", xb[:, kc * S:(kc + 1) * S],
                             xb[:, kc * S:(kc + 1) * S], r16, ALU.mult),
                   incs=[(sV, 1)] if eng == "v" else [(E.sQ, 1)])
            vnormk[kc] = (sV if eng == "v" else E.sQ, (sV if eng == "v" else E.sQ).n)
        E.lnvnorm = sV.n
        E.lnqnorm = E.sQ.n
        E.vnormk = vnormk

    qT2 = actx.enter_context(nc.sbuf_tensor("qT2", [128, 2 * S], BF16))
    kT2 = actx.enter_context(nc.sbuf_tensor("kT2", [128, 2 * S], BF16))
    vT = actx.enter_context(nc.sbuf_tensor("vT", [128, S], BF16))
    Vtok = actx.enter_context(nc.sbuf_tensor("Vtok", [128, S], BF16))
    PT = actx.enter_context(nc.sbuf_tensor("PT", [128, 16 * 512], BF16))
    OTb = actx.enter_context(nc.sbuf_tensor("OTb", [128, 2 * S], BF16))
    qsq = actx.enter_context(nc.sbuf_tensor("qsq", [128, S // 2], BF16))
    nr32 = actx.enter_context(nc.sbuf_tensor("nr32", [128, 16], F32))
    cscr = actx.enter_context(nc.sbuf_tensor("cscr", [1, 8], F32))
    nCb = actx.enter_context(nc.sbuf_tensor("nCb", [128, 2], F32))
    cs16 = actx.enter_context(nc.sbuf_tensor("cs16", [1, 2], BF16))
    linvb = actx.enter_context(nc.sbuf_tensor("linvb", [128, 512], BF16))
    lrow32 = actx.enter_context(nc.sbuf_tensor("lrow32", [1, 512], F32))
    lrow16 = actx.enter_context(nc.sbuf_tensor("lrow16", [1, 512], BF16))

    psP = actx.enter_context(nc.psum_tensor("psP", [128, 1024], F32))
    psS2a = actx.enter_context(nc.psum_tensor("psS2a", [128, 512], F32))
    psS2b = actx.enter_context(nc.psum_tensor("psS2b", [128, 512], F32))
    psOa = actx.enter_context(nc.psum_tensor("psOa", [128, 512], F32))
    psOb = actx.enter_context(nc.psum_tensor("psOb", [128, 512], F32))
    psN = actx.enter_context(nc.psum_tensor("psN", [128, 512], F32))
    psLB = actx.enter_context(nc.psum_tensor("psLB", [128, 512], F32))

    st = SimpleNamespace(evac=0, avdone=0, ot_tok={}, trtok={},
                         projtok={}, tr_last=[], wq=None,
                         avls={}, expbank={0: 0, 1: 0}, otmult={0: 0, 1: 0},
                         linv_evac=0, ncb_tok={0: 0, 1: 0},
                         ncb_war={0: 0, 1: 0}, psn_war=0, psn_bc_war=0,
                         cscr_war=0, qsq_war=0, cscr_lnexp=0, normgens=[])

    def emit_proj_mms(h, part):
        col, tp = divmod(part, 2)
        buf = wcol[col]
        for kc in range(NKC):
            for th in range(2):
                tc = tp * 2 + th
                w0 = []
                if kc == 0 and th == 0:
                    w0 = [cdma[(h, col)], (sS, st.evac)]
                    if not (h == 0 and part == 0):
                        w0 += [(sV, E.lnvnorm), (E.sQ, E.lnqnorm)]
                if h == 0 and part == 0 and th == 0:
                    w0.append(E.vnormk[kc])
                sch.op("t", call("matmul", psP[:, th * 512:(th + 1) * 512],
                                 buf[:, kc * 128:(kc + 1) * 128],
                                 xb[:, kc * S + tc * 512: kc * S + (tc + 1) * 512],
                                 start=(kc == 0), stop=(kc == NKC - 1)),
                       waits=w0,
                       incs=[(sT, 1)] if kc == NKC - 1 and th == 1 else [])
            if kc % 8 == 7 and kc != NKC - 1:
                yield
        st.projtok[(h, part)] = sT.n

    def emit_proj_part(h, part):
        for _ in emit_proj_mms(h, part):
            pass
        finish_proj_part(h, part)

    def finish_proj_part(h, part):
        col, tp = divmod(part, 2)
        hp = h % 2
        if col == 0:
            dst = qT2[:, hp * S + tp * 1024: hp * S + (tp + 1) * 1024]
            sch.op("s", call("mul", dst, psP[:], SCALE), waits=[sT], incs=[(sS, 1)])
        elif col == 1:
            dst = kT2[:, hp * S + tp * 1024: hp * S + (tp + 1) * 1024]
            sch.op("s", call("copy", dst, psP[:]), waits=[sT], incs=[(sS, 1)])
        else:
            dst = vT[:, tp * 1024:(tp + 1) * 1024]
            sch.op("s", call("copy", dst, psP[:]),
                   waits=[sT] + list(st.tr_last), incs=[(sS, 1)])
        st.evac = sS.n
    def emit_vtrans(h):
        toks = {}
        for kt in range(16):
            w = [(sS, st.evac)] if kt == 0 else []
            if kt == 0 and st.avdone:
                w.append((sT, st.avdone))
            toks[kt] = yL.dma("y", call(
                "dma_start_transpose", Vtok[:, kt * 128:(kt + 1) * 128],
                vT[:, kt * 128:(kt + 1) * 128]), waits=w)
        st.trtok[h] = toks
        st.tr_last = [toks[kt] for kt in (12, 13, 14, 15)]

    onescol = E.ones[:, 0:1]

    def norm_item(h, qk_tok):
        hp = (h % 2) * S
        hc = h % 2
        sq_pe = 0
        first = True
        for half, src in ((0, qT2), (1, kT2)):
            for piece in range(2):
                sqtok = 0
                if not DBG["norm_no_square"]:
                    if first:
                        sqw = [qk_tok]
                        if st.qsq_war:
                            sqw.append((sT, st.qsq_war))
                        first = False
                    else:
                        sqw = [(sT, sq_pe)]
                    sch.op("s", call("activation", qsq[:],
                                     src[:, hp + piece * 1024: hp + (piece + 1) * 1024],
                                     AF.Square), waits=sqw, incs=[(sS, 1)])
                    sqtok = sS.n
                    yield
                if DBG["norm_no_rounds"]:
                    continue
                for c in range(2):
                    w = []
                    if c == 0:
                        if sqtok:
                            w.append((sS, sqtok))
                        if st.psn_war:
                            w.append((sV, st.psn_war))
                        if st.psn_bc_war:
                            w.append((sV, st.psn_bc_war))
                    sch.op("t", call("matmul", psN[:, 0:512], ones[:],
                                     qsq[:, c * 512:(c + 1) * 512],
                                     start=(c == 0), stop=(c == 1)),
                           waits=w, incs=[(sT, 1)] if c == 1 else [])
                sq_pe = sT.n
                nri = half * 2 + piece
                if not DBG["norm_no_reduce"]:
                    sch.op("v", call("tensor_reduce", nr32[:, nri:nri + 1],
                                     psN[:, 0:512], AX.X, ALU.max,
                                     negate=True),
                           waits=[(sT, sq_pe)], incs=[(sV, 1)])
                    st.psn_war = sV.n
                yield
        st.qsq_war = sq_pe
        if DBG["norm_tail_const"]:
            ncw2 = [(sS, st.ncb_war[hc])] if st.ncb_war[hc] else []
            sch.op("v", call("memset", nCb[:, hc:hc + 1], -30.0),
                   waits=ncw2, incs=[(sV, 1)])
            st.ncb_tok[hc] = sV.n
            return
        sch.op("v", call("tensor_reduce", cscr[0:1, 0:1], nr32[0:1, 0:2],
                         AX.X, ALU.min),
               waits=[(sS, st.cscr_lnexp)] if st.cscr_lnexp else [])
        sch.op("v", call("tensor_reduce", cscr[0:1, 1:2], nr32[0:1, 2:4],
                         AX.X, ALU.min))
        sch.op("v", call("tensor_tensor", cscr[0:1, 2:3], cscr[0:1, 0:1],
                         cscr[0:1, 1:2], ALU.mult), incs=[(sV, 1)])
        prod_tok = sV.n
        cw = [(sV, prod_tok)]
        if st.cscr_war:
            cw.append((sT, st.cscr_war))
        sch.op("s", call("activation", cscr[0:1, 3:4], cscr[0:1, 2:3], AF.Ln),
               waits=cw, incs=[(sS, 1)])
        sch.op("s", call("activation", cscr[0:1, 4:5], cscr[0:1, 3:4], AF.Exp,
                         scale=0.5), incs=[(sS, 1)])
        c_tok = sS.n
        st.cscr_lnexp = c_tok
        yield
        cw2 = [(sS, c_tok)]
        if st.cscr_war:
            cw2.append((sT, st.cscr_war))
        sch.op("v", call("tensor_copy", cs16[0:1, 0:1], cscr[0:1, 4:5]),
               waits=cw2, incs=[(sV, 1)])
        bw = [(sV, sV.n)]
        if st.psn_bc_war:
            bw.append((sV, st.psn_bc_war))
        sch.op("t", call("matmul", psN[:, 1:2], negrow[:], cs16[0:1, 0:1],
                         start=True, stop=True), waits=bw, incs=[(sT, 1)])
        st.cscr_war = sT.n
        ncw = [(sT, st.cscr_war)]
        if st.ncb_war[hc]:
            ncw.append((sS, st.ncb_war[hc]))
        sch.op("v", call("tensor_copy", nCb[:, hc:hc + 1], psN[:, 1:2]),
               waits=ncw, incs=[(sV, 1)])
        st.ncb_tok[hc] = sV.n
        st.psn_bc_war = sV.n

    def emit_head_attn(h, qkv_ready):
        hp = (h % 2) * S
        hc = h % 2
        exp_tok = {}
        lsum_tok = {}

        def finalize(g):
            psO = psOa if g % 2 == 0 else psOb
            if DBG["no_lsum"]:
                ob = hc * S
                otw = [(sT, lsum_tok[g])]
                if g == 0 and h >= 2 and (h - 2) in st.ot_tok:
                    otw.append(st.ot_tok[h - 2])
                sch.op("v", call("tensor_copy",
                                 OTb[:, ob + g * 512: ob + (g + 1) * 512],
                                 psO[:, 0:512]), waits=otw, incs=[(sV, 1)])
                st.otmult[g % 2] = sV.n
                return
            sch.op("v", call("reciprocal", lrow32[:], psLB[0:1, 0:512]),
                   waits=[(sT, lsum_tok[g])])
            sch.op("v", call("tensor_copy", lrow16[:], lrow32[:]), incs=[(sV, 1)])
            sch.op("t", call("matmul", psLB[:, 0:512], onesrow[:], lrow16[0:1, :],
                             start=True, stop=True), waits=[(sV, sV.n)],
                   incs=[(sT, 1)])
            sch.op("v", call("tensor_copy", linvb[:], psLB[:]),
                   waits=[(sT, sT.n)], incs=[(sV, 1)])
            st.linv_evac = sV.n
            ob = hc * S
            otw = []
            if g == 0 and h >= 2 and (h - 2) in st.ot_tok:
                otw = [st.ot_tok[h - 2]]
            sch.op("v", call("tensor_tensor", OTb[:, ob + g * 512: ob + (g + 1) * 512],
                             psO[:, 0:512], linvb[:], ALU.mult),
                   waits=otw, incs=[(sV, 1)])
            st.otmult[g % 2] = sV.n

        for g in range(4):
            for kt in range(4 * g + 4):
                ps2 = psS2a if kt % 2 == 0 else psS2b
                off = max(0, (kt - 4 * g)) * 128
                w = []
                if g == 0 and kt == 0:
                    w = [(sS, qkv_ready), (sV, st.ncb_tok[hc])]
                if st.expbank[kt % 2]:
                    w.append((sS, st.expbank[kt % 2]))
                sch.op("t", call("matmul", ps2[:, off:512],
                                 kT2[:, hp + kt * 128: hp + (kt + 1) * 128],
                                 qT2[:, hp + g * 512 + off: hp + (g + 1) * 512],
                                 start=True, stop=True), waits=w, incs=[(sT, 1)])
                mm_tok = sT.n
                ew = [(sT, mm_tok)]
                if st.avls.get(kt):
                    ew.append((sT, st.avls[kt]))
                if g == 0 and kt == 0:
                    ew.append((sV, st.ncb_tok[hc]))
                if kt >= 4 * g:
                    sch.op("v", call("tensor_tensor", ps2[:, off:off + 128],
                                     ps2[:, off:off + 128], maskT[:], ALU.add),
                           waits=[(sT, mm_tok)], incs=[(sV, 1)])
                    ew.append((sV, sV.n))
                sch.op("s", call("activation", PT[:, kt * 512 + off: (kt + 1) * 512],
                                 ps2[:, off:512], AF.Exp, bias=nCb[:, hc:hc + 1]),
                       waits=ew, incs=[(sS, 1)])
                exp_tok[kt] = sS.n
                st.expbank[kt % 2] = sS.n
                st.wq.pump()
            if g > 0:
                finalize(g - 1)
            psO = psOa if g % 2 == 0 else psOb
            for kt in range(4 * g + 4):
                off = max(0, (kt - 4 * g)) * 128
                w = [(sS, exp_tok[kt])]
                if kt == 0 and st.otmult[g % 2]:
                    w.append((sV, st.otmult[g % 2]))
                tk = st.trtok[h].pop(kt, None)
                if tk is not None:
                    w.append(tk)
                sch.op("t", call("matmul", psO[:, off:512],
                                 Vtok[:, kt * 128:(kt + 1) * 128],
                                 PT[:, kt * 512 + off: (kt + 1) * 512],
                                 start=(kt == 0), stop=(kt == 4 * g + 3)),
                       waits=w, incs=[(sT, 1)])
                if not DBG["no_lsum"]:
                    lw = []
                    if kt == 0 and st.linv_evac:
                        lw.append((sV, st.linv_evac))
                    sch.op("t", call("matmul", psLB[0:1, off:512], onescol,
                                     PT[:, kt * 512 + off: (kt + 1) * 512],
                                     start=(kt == 0), stop=(kt == 4 * g + 3)),
                           waits=lw, incs=[(sT, 1)])
                st.avls[kt] = sT.n
                if kt % 2 == 1:
                    st.wq.pump()
            lsum_tok[g] = sT.n
            st.avdone = sT.n
            st.wq.pump()
        st.wq.pump()
        finalize(3)
        st.ncb_war[hc] = sS.n
        st.ot_tok[h] = E.ot_store(h, OTb[:, hc * S:(hc + 1) * S], [(sV, sV.n)])

    def proj_item(h):
        for part in range(6):
            for _ in emit_proj_mms(h, part):
                yield
            finish_proj_part(h, part)
            if part == 3 and not DBG["const_c"]:
                st.normgens.append(norm_item(h, (sS, st.evac)))
            if part in (1, 3, 5) and h + 1 < HPC:
                dma_col(h + 1, part // 2, waits=[(sT, st.projtok[(h, part)])])
            hook = getattr(E, "proj_hook", None)
            if hook is not None:
                hook(h, part, (sT, st.projtok[(h, part)]))
            yield

    def vtrans_item(h):
        emit_vtrans(h)
        yield

    class WorkQueue:

        def __init__(self):
            self.items = []

        def push(self, kind, gen):
            self.items.append((kind, gen))

        def push_front(self, kind, gen):
            self.items.insert(0, (kind, gen))

        def pump(self):
            while self.items:
                kind, gen = self.items[0]
                try:
                    next(gen)
                    return
                except StopIteration:
                    self.items.pop(0)

        def drain_kind(self, kinds):
            while self.items and self.items[0][0] in kinds:
                kind, gen = self.items[0]
                try:
                    next(gen)
                except StopIteration:
                    self.items.pop(0)

        def drain_where(self, kind):
            for i in range(len(self.items) - 1, -1, -1):
                if self.items[i][0] == kind:
                    gen = self.items[i][1]
                    while True:
                        try:
                            next(gen)
                        except StopIteration:
                            break
                    self.items.pop(i)

        def drain(self):
            while self.items:
                self.pump()

    wq = WorkQueue()
    st.wq = wq

    for _ in proj_item(0):
        pass
    emit_vtrans(0)
    if DBG["const_c"]:
        sch.op("v", call("memset", nCb[:, 0:2], -30.0), incs=[(sV, 1)])
        st.ncb_tok[0] = st.ncb_tok[1] = sV.n

    for h in range(HPC):
        qkv_ready = st.evac
        while st.normgens:
            for _ in st.normgens.pop(0):
                pass
        wq.drain_where("vtrans")
        if h + 1 < HPC:
            wq.push("proj", proj_item(h + 1))
        emit_head_attn(h, qkv_ready)
        wq.drain_where("proj")
        if h + 1 < HPC:
            wq.items.insert(min(1, len(wq.items)), ("vtrans", vtrans_item(h + 1)))
        else:
            wq.drain()
    E.projlast = (E.sT, st.projtok[(HPC - 1, 5)])


def _phaseBC(E):
    sch, nc, ctx = E.sch, E.nc, E.ctx
    sT, sV, sS, sQ = E.sT, E.sV, E.sS, E.sQ
    yL, pL, sL, qL, oF = E.yL, E.pL, E.sL, E.qL, E.oF
    ones = E.ones
    xin, wo_sw, WT_sw, A_sw, B_sw, bias_sw, probs_sw = (
        E.xin, E.wo_sw, E.WT_sw, E.A_sw, E.B_sw, E.bias_sw, E.probs_sw)
    outT, attnP, xres, ffnX = E.outT, E.attnP, E.xres, E.ffnX
    gate = list(E.gate)

    def lnfin(m_sl, r_sl, ps_sum, ps_sq):
        sch.op("v", call("tensor_scalar", m_sl, ps_sum[:], 1.0 / D, None, ALU.mult),
               waits=[(sT, sT.n)], incs=[(sV, 1)])
        sch.op("v", call("tensor_scalar", tmp[:], ps_sq[:], 1.0 / D, None, ALU.mult),
               incs=[(sV, 1)])
        sch.op("s", call("activation", r_sl, m_sl, AF.Square), waits=[sV], incs=[(sS, 1)])
        sch.op("v", call("tensor_tensor", tmp[:], tmp[:], r_sl, ALU.subtract),
               waits=[sS])
        sch.op("v", call("tensor_scalar", tmp[:], tmp[:], EPS, None, ALU.add),
               incs=[(sV, 1)])
        sch.op("s", call("activation", tmp[:], tmp[:], AF.Ln),
               waits=[sV], incs=[(sS, 1)])
        sch.op("s", call("activation", r_sl, tmp[:], AF.Exp, scale=-0.5),
               incs=[(sS, 1)])
        return (sS, sS.n)

    wbuf = getattr(E, "wbuf", None)
    if wbuf is None:
        wbuf = ctx.enter_context(nc.sbuf_tensor("wbuf", [128, 2 * D], BF16))
    aP = ctx.enter_context(nc.sbuf_tensor("aP", [128, 4 * NOWN], F32))
    x16 = ctx.enter_context(nc.sbuf_tensor("x16", [128, 2 * NOWN], BF16))
    sq16 = ctx.enter_context(nc.sbuf_tensor("sq16", [128, 2 * NOWN], BF16))
    xinb = ctx.enter_context(nc.sbuf_tensor("xinb", [128, 4 * NOWN], F32))
    xrb = ctx.enter_context(nc.sbuf_tensor("xrb", [128, 4 * NOWN], F32))
    m_a = ctx.enter_context(nc.sbuf_tensor("m_a", [128, NOWN], F32))
    r_a = ctx.enter_context(nc.sbuf_tensor("r_a", [128, NOWN], F32))
    m_2 = ctx.enter_context(nc.sbuf_tensor("m_2", [128, NOWN], F32))
    r_2 = ctx.enter_context(nc.sbuf_tensor("r_2", [128, NOWN], F32))
    m_3 = ctx.enter_context(nc.sbuf_tensor("m_3", [128, NOWN], F32))
    r_3 = ctx.enter_context(nc.sbuf_tensor("r_3", [128, NOWN], F32))
    tmp = aP[:, 0:NOWN]
    fb16 = ctx.enter_context(nc.sbuf_tensor("fb16", [128, 4 * NOWN], BF16))
    mr16 = fb16
    bias_sb = ctx.enter_context(nc.sbuf_tensor("bias_sb", [128, NKC], F32))
    probs_sb = ctx.enter_context(nc.sbuf_tensor("probs_sb", [128, 2], F32))

    psW = ctx.enter_context(nc.psum_tensor("psW", [128, 2048], F32))
    psSum = ctx.enter_context(nc.psum_tensor("psSumB", [128, 1024], F32))
    psSq = ctx.enter_context(nc.psum_tensor("psSqB", [128, 1024], F32))

    bscope = ExitStack()
    OTsb = bscope.enter_context(nc.sbuf_tensor("OTsb", [128, H * NOWN], BF16))

    otf_lane = yL if E.otf_eng == "y" else qL
    ot_own = otf_lane.dma(E.otf_eng, lambda e: e.dma_start(
        out=OTsb[:, 0:HPC * NOWN].rearrange("p (hv t) -> p hv t", hv=HPC),
        in_=E.otf_own(e)), waits=E.own_gate)
    ot_mate = otf_lane.dma(E.otf_eng, lambda e: e.dma_start(
        out=OTsb[:, HPC * NOWN:].rearrange("p (hv t) -> p hv t", hv=HPC),
        in_=E.otf_mate(e)), waits=gate)
    sch.op("y", call("dma_start", out=bias_sb[:], in_=bias_sw[:]),
           incs=[(E.dC, 16)])
    sch.op("y", call("dma_start", out=probs_sb[:], in_=probs_sw[:]),
           incs=[(E.dC, 16)])
    cn = (E.dC, E.dC.n)

    wtok = dict(getattr(E, "pre_wtok", {}))
    slot_last = {}
    sp_at, sp_xr, sp_ff = {}, {}, {}
    statT = {}
    fin_done = {}
    stat_q = []

    def emit_stats(ps_sum, ps_sq, ph, dc, pg, vtok, stok):
        if ph == "l2":
            xsrc = [xres16[:, dc * NOWN + tc * 512: dc * NOWN + (tc + 1) * 512]
                    for tc in range(2)]
        else:
            xsrc = [x16[:, pg * NOWN + tc * 512: pg * NOWN + (tc + 1) * 512]
                    for tc in range(2)]
        for tc in range(2):
            sch.op("t", call("matmul", ps_sum[:, tc * 512:(tc + 1) * 512], ones[:],
                             xsrc[tc], start=(dc == 0), stop=(dc == NKC - 1)),
                   waits=[vtok, stok] if tc == 0 else [])
            sch.op("t", call("matmul", ps_sq[:, tc * 512:(tc + 1) * 512], ones[:],
                             sq16[:, pg * NOWN + tc * 512: pg * NOWN + (tc + 1) * 512],
                             start=(dc == 0), stop=(dc == NKC - 1)),
                   incs=[(sT, 1)] if tc == 1 else [])
        statT[(ph, dc)] = sT.n

    for si0 in (0, 1):
        if si0 not in wtok:
            wtok[si0] = qL.dma("q", call(
                "dma_start", out=wbuf[:, si0 * D:(si0 + 1) * D],
                in_=wo_sw[si0 * 128:(si0 + 1) * 128, :]), waits=gate)
    for dc in range(NKC):
        pg = dc % 2
        if dc + 1 < NKC and (dc + 1) not in wtok:
            wtok[dc + 1] = qL.dma("q", call(
                "dma_start", out=wbuf[:, (1 - pg) * D:(2 - pg) * D],
                in_=wo_sw[(dc + 1) * 128:(dc + 2) * 128, :]), waits=[(sT, sT.n)])
        for hv in range(H):
            for tc in range(2):
                w0 = []
                if hv == 0 and tc == 0:
                    w0 = [wtok[dc], ot_own, cn]
                    lk = slot_last.pop(("psW", pg), None)
                    if lk is not None:
                        w0.extend(lk if isinstance(lk, list) else [lk])
                if hv == HPC and tc == 0:
                    w0 = [ot_mate]
                sch.op("t", call("matmul", psW[:, (2 * pg + tc) * 512:(2 * pg + tc + 1) * 512],
                                 wbuf[:, pg * D + hv * 128: pg * D + (hv + 1) * 128],
                                 OTsb[:, hv * NOWN + tc * 512: hv * NOWN + (tc + 1) * 512],
                                 start=(hv == 0), stop=(hv == H - 1)),
                       waits=w0, incs=[(sT, 1)] if hv == H - 1 and tc == 1 else [])
        sl = aP[:, (dc % 4) * NOWN:((dc % 4) + 1) * NOWN]
        ew = [sT]
        lk = slot_last.pop(("aP", dc % 4), None)
        if lk is not None:
            ew.extend(lk if isinstance(lk, list) else [lk])
        sch.op("v", call("tensor_copy", sl, psW[:, 2 * pg * 512: 2 * pg * 512 + 1024]),
               waits=ew, incs=[(sV, 1)])
        ev = (sV, sV.n)
        slot_last[("psW", pg)] = ev
        sw_ = [sV] if dc < 2 else [sV, (sT, statT[("wo", dc - 2)])]
        sch.op("s", call("activation", sq16[:, pg * NOWN:(pg + 1) * NOWN], sl,
                         AF.Square), waits=sw_, incs=[(sS, 1)])
        sqv = (sS, sS.n)
        xw16 = [] if dc < 2 else [(sT, statT[("wo", dc - 2)])]
        lkx = slot_last.pop(("x16", pg), None)
        if lkx is not None:
            xw16.append(lkx)
        sch.op("v", call("tensor_copy", x16[:, pg * NOWN:(pg + 1) * NOWN], sl),
               waits=xw16, incs=[(sV, 1)])
        xv = (sV, sV.n)
        stat_q.append(("wo", dc, pg, xv, sqv))
        if dc > 0:
            emit_stats(psSum, psSq, *stat_q.pop(0))
        sp_at[dc] = pL.dma("y", call("dma_start",
                                     out=attnP[dc * 128:(dc + 1) * 128, :],
                                     in_=x16[:, pg * NOWN:(pg + 1) * NOWN]),
                           waits=[xv])
        slot_last[("x16", pg)] = sp_at[dc]
        slot_last[("aP", dc % 4)] = [xv, sqv]
    emit_stats(psSum, psSq, *stat_q.pop(0))
    wo_last = (sT, sT.n)
    ma_tok = lnfin(m_a[:], r_a[:], psSum, psSq)
    bscope.close()
    xres16 = ctx.enter_context(nc.sbuf_tensor("xres16", [128, NKC * NOWN], BF16))
    A_sb = ctx.enter_context(nc.sbuf_tensor("A_sb", [128, NKC * 256], BF16))
    B_sb = ctx.enter_context(nc.sbuf_tensor("B_sb", [128, 2 * D], BF16))
    hs16 = ctx.enter_context(nc.sbuf_tensor("hs16", [128, 2 * NOWN], BF16))
    atok = qL.dma("q", call("dma_start", out=A_sb[:], in_=A_sw[:]), waits=[wo_last])
    btok = qL.dma("q", call("dma_start", out=B_sb[:], in_=B_sw[:]), waits=[wo_last])
    wtok[NKC] = qL.dma("q", call("dma_start", out=wbuf[:, 0:D],
                                 in_=WT_sw[0:128, :]), waits=[wo_last])
    wtok[NKC + 1] = qL.dma("q", call("dma_start", out=wbuf[:, D:2 * D],
                                     in_=WT_sw[128:256, :]), waits=[wo_last])

    resid_tok = {}
    rload = {}

    def resid_load(dc):
        s4 = dc % 4
        w = [sp_at[dc]]
        lk = slot_last.pop(("fb16", s4), None)
        if lk is not None:
            w.extend(lk if isinstance(lk, list) else [lk])
        atk = sL.dma("s", call("dma_start", out=fb16[:, s4 * NOWN:(s4 + 1) * NOWN],
                               in_=attnP[dc * 128:(dc + 1) * 128, :]), waits=w)
        xw = []
        lk = slot_last.pop(("xinb", s4), None)
        if lk is not None:
            xw.extend(lk if isinstance(lk, list) else [lk])
        xtk = yL.dma("y", call("dma_start", out=xinb[:, s4 * NOWN:(s4 + 1) * NOWN],
                               in_=xin[dc * 128:(dc + 1) * 128, :]), waits=xw)
        rload[dc] = (atk, xtk)

    def resid_chunk_ops(dc):
        s4 = dc % 4
        sl = aP[:, s4 * NOWN:(s4 + 1) * NOWN]
        atk, xtk = rload.pop(dc)
        xr = xrb[:, s4 * NOWN:(s4 + 1) * NOWN]
        lk = slot_last.pop(("xrb", s4), None)
        lk = [] if lk is None else (lk if isinstance(lk, list) else [lk])
        lka = slot_last.pop(("aP", s4), None)
        lka = [] if lka is None else (lka if isinstance(lka, list) else [lka])
        steps = []
        for eng, c0, c1 in (("v", 0, 640), ("q", 640, 1024)):
            w0 = [atk, ma_tok] + lka + ([wo_last] if dc == 0 and eng == "q" else [])
            steps.append((eng, call("tensor_tensor", sl[:, c0:c1],
                                    fb16[:, s4 * NOWN + c0: s4 * NOWN + c1],
                                    m_a[:, c0:c1], ALU.subtract), w0, []))
            steps.append((eng, call("tensor_tensor", sl[:, c0:c1], sl[:, c0:c1],
                                    r_a[:, c0:c1], ALU.mult), [], []))
            w1 = [xtk] + lk
            steps.append((eng, call("tensor_tensor", xr[:, c0:c1], sl[:, c0:c1],
                                    xinb[:, s4 * NOWN + c0: s4 * NOWN + c1],
                                    ALU.add), w1, []))
            steps.append((eng, call("tensor_copy",
                                    xres16[:, dc * NOWN + c0: dc * NOWN + c1],
                                    xr[:, c0:c1]), [], ["tok"]))
        return steps

    def emit_pairwise(step_lists, fini):
        idx = [0] * len(step_lists)
        toks = [[] for _ in step_lists]
        done = 0
        while done < len(step_lists):
            done = 0
            for ci, steps in enumerate(step_lists):
                if idx[ci] >= len(steps):
                    done += 1
                    continue
                eng, fn, ws, mark = steps[idx[ci]]
                sem = sV if eng == "v" else sQ
                sch.op(eng, fn, waits=ws,
                       incs=[(sem, 1)] if mark else [])
                if mark:
                    toks[ci].append((sem, sem.n))
                idx[ci] += 1
        for ci, f in enumerate(fini):
            f(toks[ci])

    def resid_fini(dc):
        s4 = dc % 4
        xr = xrb[:, s4 * NOWN:(s4 + 1) * NOWN]

        def f(toks):
            resid_tok[dc] = toks
            sp_xr[dc] = E.vL.dma("q", call(
                "dma_start", out=xres[dc * 128:(dc + 1) * 128, :], in_=xr),
                waits=toks)
            slot_last[("xrb", s4)] = sp_xr[dc]
            slot_last[("xinb", s4)] = list(toks)
            slot_last[("fb16", s4)] = list(toks)
            slot_last[("aP", s4)] = list(toks)
            sw_ = list(toks) + ([] if dc < 2 else [(sT, statT[("l2", dc - 2)])])
            sch.op("s", call("activation",
                             sq16[:, (dc % 2) * NOWN:((dc % 2) + 1) * NOWN],
                             xr, AF.Square), waits=sw_, incs=[(sS, 1)])
            stat_q.append(("l2", dc, dc % 2, toks[-1], (sS, sS.n)))
        return f

    def l2_stats(d2):
        ph, _, pg2, vt, st_ = stat_q.pop(0)
        rt = resid_tok[d2]
        sch.op("t", call("matmul", psSum[:, 0:512], ones[:],
                         xres16[:, d2 * NOWN: d2 * NOWN + 512],
                         start=(d2 == 0), stop=(d2 == NKC - 1)),
               waits=[rt[0], rt[1], st_])
        sch.op("t", call("matmul", psSum[:, 512:1024], ones[:],
                         xres16[:, d2 * NOWN + 512: d2 * NOWN + 1024],
                         start=(d2 == 0), stop=(d2 == NKC - 1)))
        sch.op("t", call("matmul", psSq[:, 0:512], ones[:],
                         sq16[:, pg2 * NOWN: pg2 * NOWN + 512],
                         start=(d2 == 0), stop=(d2 == NKC - 1)))
        sch.op("t", call("matmul", psSq[:, 512:1024], ones[:],
                         sq16[:, pg2 * NOWN + 512: pg2 * NOWN + 1024],
                         start=(d2 == 0), stop=(d2 == NKC - 1)),
               incs=[(sT, 1)])
        statT[("l2", d2)] = sT.n

    for dpre in range(3):
        resid_load(dpre)
    for dc in range(0, NKC, 2):
        while stat_q and stat_q[0][1] <= dc - 1:
            l2_stats(stat_q[0][1])
        emit_pairwise([resid_chunk_ops(dc), resid_chunk_ops(dc + 1)],
                      [resid_fini(dc), resid_fini(dc + 1)])
        for dn in (dc + 3, dc + 4):
            if dn < NKC:
                resid_load(dn)
    while stat_q:
        l2_stats(stat_q[0][1])
    m2_tok = lnfin(m_2[:], r_2[:], psSum, psSq)
    sch.op("v", call("tensor_copy", mr16[:, 0:NOWN], m_2[:]), waits=[m2_tok])
    sch.op("v", call("tensor_copy", mr16[:, NOWN:2 * NOWN], r_2[:]), incs=[(sV, 1)])
    mr_tok = (sV, sV.n)

    ln2t_tok = {}
    for dc in range(NKC):
        toks = []
        for eng, c0, c1 in (("v", 0, 640), ("q", 640, 1024)):
            xs = xres16[:, dc * NOWN + c0: dc * NOWN + c1]
            sch.op(eng, call("tensor_tensor", xs, xs, mr16[:, c0:c1], ALU.subtract),
                   waits=[mr_tok, (sT, statT[("l2", NKC - 1)])])
            sch.op(eng, call("tensor_tensor", xs, xs, mr16[:, NOWN + c0: NOWN + c1],
                             ALU.mult),
                   incs=[(sV, 1)] if eng == "v" else [(sQ, 1)])
            toks.append((sV, sV.n) if eng == "v" else (sQ, sQ.n))
        ln2t_tok[dc] = toks
    LN2T = xres16
    for dc in range(NKC):
        for erc in range(2):
            for tc in range(2):
                w0 = []
                if tc == 0 and erc == 0:
                    w0 = list(ln2t_tok[dc]) + ([atok, btok, cn] if dc == 0 else [])
                    if dc == 0:
                        for kk in (("psW", 0), ("psW", 1)):
                            lk = slot_last.pop(kk, None)
                            if lk is not None:
                                w0.extend(lk if isinstance(lk, list) else [lk])
                sch.op("t", call("matmul",
                                 psW[:, (2 * erc + tc) * 512:(2 * erc + tc + 1) * 512],
                                 A_sb[:, dc * 256 + erc * 128: dc * 256 + (erc + 1) * 128],
                                 LN2T[:, dc * NOWN + tc * 512: dc * NOWN + (tc + 1) * 512],
                                 start=(dc == 0), stop=(dc == NKC - 1)),
                       waits=w0,
                       incs=[(sT, 1)] if dc == NKC - 1 and erc == 1 and tc == 1 else [])
    for erc in range(2):
        sch.op("v", call("tensor_scalar", hs16[:, erc * NOWN:(erc + 1) * NOWN],
                         psW[:, 2 * erc * 512: 2 * erc * 512 + 1024],
                         probs_sb[:, erc:erc + 1], None, ALU.mult),
               waits=[sT], incs=[(sV, 1)])
    hs_done = (sV, sV.n)
    slot_last[("psW", 0)] = hs_done
    slot_last[("psW", 1)] = hs_done

    for oc in range(NKC):
        pg = oc % 2
        sio = NKC + oc
        if oc + 1 < NKC and (sio + 1) not in wtok:
            wtok[sio + 1] = qL.dma("q", call(
                "dma_start", out=wbuf[:, (1 - pg) * D:(2 - pg) * D],
                in_=WT_sw[(oc + 1) * 128:(oc + 2) * 128, :]), waits=[(sT, sT.n)])
        for dc in range(NKC):
            for tc in range(2):
                w0 = []
                if dc == 0 and tc == 0:
                    w0 = [wtok[sio], hs_done]
                    lk = slot_last.pop(("psW", pg), None)
                    if lk is not None:
                        w0.extend(lk if isinstance(lk, list) else [lk])
                sch.op("t", call("matmul",
                                 psW[:, (2 * pg + tc) * 512:(2 * pg + tc + 1) * 512],
                                 wbuf[:, pg * D + dc * 128: pg * D + (dc + 1) * 128],
                                 LN2T[:, dc * NOWN + tc * 512: dc * NOWN + (tc + 1) * 512],
                                 start=(dc == 0), stop=False), waits=w0)
        for erc in range(2):
            for tc in range(2):
                sch.op("t", call("matmul",
                                 psW[:, (2 * pg + tc) * 512:(2 * pg + tc + 1) * 512],
                                 B_sb[:, erc * D + oc * 128: erc * D + (oc + 1) * 128],
                                 hs16[:, erc * NOWN + tc * 512: erc * NOWN + (tc + 1) * 512],
                                 start=False, stop=(erc == 1)),
                       incs=[(sT, 1)] if erc == 1 and tc == 1 else [])
        s4 = oc % 4
        sl = aP[:, s4 * NOWN:(s4 + 1) * NOWN]
        w = [sT]
        lk = slot_last.pop(("aP", s4), None)
        if lk is not None:
            w.extend(lk if isinstance(lk, list) else [lk])
        if (oc - 2) in sp_ff:
            w.append(sp_ff[oc - 2])
        sch.op("v", call("tensor_scalar", sl, psW[:, 2 * pg * 512: 2 * pg * 512 + 1024],
                         bias_sb[:, oc:oc + 1], None, ALU.add),
               waits=w, incs=[(sV, 1)])
        ev = (sV, sV.n)
        slot_last[("psW", pg)] = ev
        sw_ = [sV] if oc < 2 else [sV, (sT, statT[("l3", oc - 2)])]
        sch.op("s", call("activation", sq16[:, pg * NOWN:(pg + 1) * NOWN], sl,
                         AF.Square), waits=sw_, incs=[(sS, 1)])
        sqv = (sS, sS.n)
        xw16 = [] if oc < 2 else [(sT, statT[("l3", oc - 2)])]
        lkx = slot_last.pop(("x16", pg), None)
        if lkx is not None:
            xw16.append(lkx)
        sch.op("v", call("tensor_copy", x16[:, pg * NOWN:(pg + 1) * NOWN], sl),
               waits=xw16, incs=[(sV, 1)])
        xv = (sV, sV.n)
        stat_q.append(("l3", oc, pg, xv, sqv))
        if oc > 0:
            emit_stats(psSum, psSq, *stat_q.pop(0))
        sp_ff[oc] = pL.dma("y", call("dma_start",
                                     out=ffnX[oc * 128:(oc + 1) * 128, :],
                                     in_=x16[:, pg * NOWN:(pg + 1) * NOWN]),
                           waits=[xv])
        slot_last[("x16", pg)] = sp_ff[oc]
        slot_last[("aP", s4)] = [xv, sqv]
    emit_stats(psSum, psSq, *stat_q.pop(0))
    m3_tok = lnfin(m_3[:], r_3[:], psSum, psSq)

    fload = {}

    def fin_load(dc):
        s4 = dc % 4
        w = [sp_ff[dc]]
        lk = slot_last.pop(("fb16", s4), None)
        if lk is not None:
            w.extend(lk if isinstance(lk, list) else [lk])
        ftk = E.vL.dma("q", call("dma_start", out=fb16[:, s4 * NOWN:(s4 + 1) * NOWN],
                                 in_=ffnX[dc * 128:(dc + 1) * 128, :]), waits=w)
        xr = xrb[:, s4 * NOWN:(s4 + 1) * NOWN]
        xw = [sp_xr[dc]]
        lk = slot_last.pop(("xrb", s4), None)
        if lk is not None:
            xw.extend(lk if isinstance(lk, list) else [lk])
        rtk = sL.dma("s", call("dma_start", out=xr,
                               in_=xres[dc * 128:(dc + 1) * 128, :]), waits=xw)
        fload[dc] = (ftk, rtk)

    def fin_chunk_ops(dc):
        s4 = dc % 4
        sl = aP[:, s4 * NOWN:(s4 + 1) * NOWN]
        xr = xrb[:, s4 * NOWN:(s4 + 1) * NOWN]
        ftk, rtk = fload.pop(dc)
        lka = slot_last.pop(("aP", s4), None)
        lka = [] if lka is None else (lka if isinstance(lka, list) else [lka])
        steps = []
        for eng, c0, c1 in (("v", 0, 640), ("q", 640, 1024)):
            steps.append((eng, call("tensor_tensor", sl[:, c0:c1],
                                    fb16[:, s4 * NOWN + c0: s4 * NOWN + c1],
                                    m_3[:, c0:c1], ALU.subtract),
                          [ftk, m3_tok] + lka, []))
            steps.append((eng, call("tensor_tensor", sl[:, c0:c1], sl[:, c0:c1],
                                    r_3[:, c0:c1], ALU.mult), [], []))
            steps.append((eng, call("tensor_tensor", xr[:, c0:c1], sl[:, c0:c1],
                                    xr[:, c0:c1], ALU.add), [rtk], ["tok"]))
        return steps

    def fin_fini(dc):
        s4 = dc % 4
        xr = xrb[:, s4 * NOWN:(s4 + 1) * NOWN]

        def f(toks):
            fin_done[dc] = toks
            tok = oF.dma("y", call("dma_start",
                                   out=outT[dc * 128:(dc + 1) * 128, :], in_=xr),
                         waits=toks)
            slot_last[("xrb", s4)] = tok
            slot_last[("fb16", s4)] = list(toks)
            slot_last[("aP", s4)] = list(toks)
        return f

    for dpre in range(3):
        fin_load(dpre)
    for dc in range(0, NKC, 2):
        emit_pairwise([fin_chunk_ops(dc), fin_chunk_ops(dc + 1)],
                      [fin_fini(dc), fin_fini(dc + 1)])
        for dn in (dc + 3, dc + 4):
            if dn < NKC:
                fin_load(dn)
    sch.op("y", call("nop"), waits=oF.totals())


def _mk_env(nc, sch, ctx):
    E = SimpleNamespace(nc=nc, sch=sch, ctx=ctx)
    E.dC = sch.sem("dC")
    E.yL = Lanes(sch, "yL", 4)
    E.qL = Lanes(sch, "qL", 2)
    E.oL = Lanes(sch, "oL", 2)
    E.pL = Lanes(sch, "pL", 4)
    E.sL = Lanes(sch, "sL", 4)
    E.vL = Lanes(sch, "vL", 2)
    E.oF = Lanes(sch, "oF", 2)
    E.sT = sch.sem("sT")
    E.sV = sch.sem("sV")
    E.sS = sch.sem("sS")
    E.sQ = sch.sem("sQ")
    sch.chain = {"v": E.sV, "s": E.sS, "q": E.sQ}
    return E


def _load_consts(E, srcs):
    nc, sch = E.nc, E.sch
    E.ident = E.ctx.enter_context(nc.sbuf_tensor("ident", [128, 128], BF16))
    E.ones = E.ctx.enter_context(nc.sbuf_tensor("ones", [128, 128], BF16))
    E.onesrow = E.ctx.enter_context(nc.sbuf_tensor("onesrow", [1, 128], BF16))
    E.negrow = E.ctx.enter_context(nc.sbuf_tensor("negrow", [1, 128], BF16))
    E.mask = E.ctx.enter_context(nc.sbuf_tensor("mask", [128, 128], BF16))
    E.maskT = E.ctx.enter_context(nc.sbuf_tensor("maskT", [128, 128], BF16))
    for dst, src in ((E.ident, srcs["identA"]), (E.ones, srcs["onesA"]),
                     (E.onesrow, srcs["onesrowA"]), (E.negrow, srcs["negrowA"]),
                     (E.mask, srcs["maskA"]), (E.maskT, srcs["maskTA"])):
        sch.op("y", call("dma_start", out=dst[:], in_=src[:]), incs=[(E.dC, 16)])
    E.consts = (E.dC, E.dC.n)


def _declare_const_drams(nc):
    return {
        "identA": nc.dram_tensor("identA", [128, 128], BF16, kind="ExternalInput").ap(),
        "onesA": nc.dram_tensor("onesA", [128, 128], BF16, kind="ExternalInput").ap(),
        "onesrowA": nc.dram_tensor("onesrowA", [1, 128], BF16, kind="ExternalInput").ap(),
        "negrowA": nc.dram_tensor("negrowA", [1, 128], BF16, kind="ExternalInput").ap(),
        "maskA": nc.dram_tensor("maskA", [128, 128], BF16, kind="ExternalInput").ap(),
        "maskTA": nc.dram_tensor("maskTA", [128, 128], BF16, kind="ExternalInput").ap(),
    }


def _declare_bc_drams(nc, internal_otf):
    d = {}
    if not internal_otf:
        d["OTf"] = nc.dram_tensor("OTf", [H * 128, NOWN], BF16, kind="ExternalInput").ap()
    d["xin"] = nc.dram_tensor("xin", [D, NOWN], F32, kind="ExternalInput").ap()
    d["wo_sw"] = nc.dram_tensor("wo_sw", [NKC * 128, D], F32, kind="ExternalInput").ap()
    d["WT_sw"] = nc.dram_tensor("WT_sw", [NKC * 128, D], F32, kind="ExternalInput").ap()
    d["A_sw"] = nc.dram_tensor("A_sw", [128, NKC * 256], F32, kind="ExternalInput").ap()
    d["B_sw"] = nc.dram_tensor("B_sw", [128, 2 * D], F32, kind="ExternalInput").ap()
    d["bias_sw"] = nc.dram_tensor("bias_sw", [128, NKC], F32, kind="ExternalInput").ap()
    d["probs_sw"] = nc.dram_tensor("probs_sw", [128, 2], F32, kind="ExternalInput").ap()
    d["outT"] = nc.dram_tensor("outT", [D, NOWN], F32, kind="ExternalOutput").ap()
    d["attnP"] = nc.dram_tensor("attnP", [D, NOWN], BF16, kind="Internal").ap()
    d["xres"] = nc.dram_tensor("xres", [D, NOWN], F32, kind="Internal").ap()
    d["ffnX"] = nc.dram_tensor("ffnX", [D, NOWN], BF16, kind="Internal").ap()
    return d


def build_fused():
    nc = bass.Bass("TRN2", target_bir_lowering=False)
    hsT = nc.dram_tensor("hsT", [D, S], F32, kind="ExternalInput").ap()
    wqkv = nc.dram_tensor("wqkv", [HPC * 128, NKC * 384], F32, kind="ExternalInput").ap()
    csrc = _declare_const_drams(nc)
    CCsh = nc.dram_tensor("CCsh", [H * 128, S], BF16, kind="Internal",
                          addr_space="Shared").ap()
    bbin = nc.dram_tensor("bbin", [1, 16], F32, kind="Internal").ap()
    bbout = nc.dram_tensor("bbout", [8, 16], F32, kind="Internal",
                           addr_space="Shared").ap()
    bc = _declare_bc_drams(nc, internal_otf=True)

    sch = Sched()
    ctx = ExitStack()
    with ctx:
        E = _mk_env(nc, sch, ctx)
        cc = sch.sem("cc")
        _load_consts(E, csrc)

        base_cell = {}

        def getpid(e):
            if "pid" not in base_cell:
                base_cell["pid"] = e.partition_id()
            return base_cell["pid"]

        def ot_store(h, src, waits):
            def fn(e, h=h, src=src):
                if "v" not in base_cell:
                    base_cell["v"] = (getpid(e) % 2) * (HPC * 128)
                dst = CCsh[bass.DynSlice(base_cell["v"], HPC * 128), :]
                return e.dma_start(out=dst[h * 128:(h + 1) * 128, :], in_=src)
            return E.oL.dma("q", fn, waits=waits)
        E.ot_store = ot_store
        E.hsT, E.wqkv = hsT, wqkv
        E.pre_wtok = {}

        def proj_hook(hh, p, tok):
            if hh == HPC - 1 and p == 5:
                E.pre_wtok[0] = E.qL.dma("q", lambda e: e.dma_start(
                    out=E.wbuf[:, 0:D], in_=bc["wo_sw"][0:128, :]), waits=[tok])
                E.pre_wtok[1] = E.qL.dma("q", lambda e: e.dma_start(
                    out=E.wbuf[:, D:2 * D], in_=bc["wo_sw"][128:256, :]), waits=[tok])
        E.proj_hook = proj_hook
        ascope = ExitStack()
        _phaseA(E, ascope)
        ascope.close()
        wbuf = ctx.enter_context(nc.sbuf_tensor("wbuf", [128, 2 * D], BF16))
        E.wbuf = wbuf
        btok = E.qL.dma("q", call("dma_start", out=bbin[:], in_=E.ones[0:1, 0:16]))
        sch.op("q", call("collective_compute", "AllGather", ALU.bypass,
                         [list(range(8))], [bbin.opt()], [bbout.opt()]),
               waits=E.oL.totals() + [btok], incs=[(cc, 1)], chainable=False)

        def otf_own(e):
            pid = getpid(e)
            return CCsh[bass.DynSlice((pid % 2) * (HPC * 128), HPC * 128),
                        bass.DynSlice((pid % 2) * NOWN, NOWN)].rearrange(
                "(hv p) t -> p hv t", p=128)

        def otf_mate(e):
            pid = getpid(e)
            return CCsh[bass.DynSlice(((pid + 1) % 2) * (HPC * 128), HPC * 128),
                        bass.DynSlice((pid % 2) * NOWN, NOWN)].rearrange(
                "(hv p) t -> p hv t", p=128)
        E.otf_own, E.otf_mate = otf_own, otf_mate
        E.otf_eng = "q"
        E.own_gate = E.oL.totals()
        E.gate = [(cc, 1)]
        for k in ("xin", "wo_sw", "WT_sw", "A_sw", "B_sw", "bias_sw", "probs_sw",
                  "outT", "attnP", "xres", "ffnX"):
            setattr(E, k, bc[k])
        _phaseBC(E)
        sch.emit(nc, ctx)
    return nc


def build_progA():
    nc = bass.Bass("TRN2", target_bir_lowering=False)
    hsT = nc.dram_tensor("hsT", [D, S], F32, kind="ExternalInput").ap()
    wqkv = nc.dram_tensor("wqkv", [HPC * 128, NKC * 384], F32, kind="ExternalInput").ap()
    csrc = _declare_const_drams(nc)
    OT = nc.dram_tensor("OT", [HPC * 128, S], BF16, kind="ExternalOutput").ap()
    sch = Sched()
    ctx = ExitStack()
    with ctx:
        E = _mk_env(nc, sch, ctx)
        _load_consts(E, csrc)

        def ot_store(h, src, waits):
            return E.oL.dma("y", call("dma_start", out=OT[h * 128:(h + 1) * 128, :],
                                      in_=src), waits=waits)
        E.ot_store = ot_store
        E.hsT, E.wqkv = hsT, wqkv
        ascope = ExitStack()
        _phaseA(E, ascope)
        sch.op("y", call("nop"), waits=E.oL.totals())
        ascope.close()
        sch.emit(nc, ctx)
    return nc


def build_progB():
    nc = bass.Bass("TRN2", target_bir_lowering=False)
    csrc = _declare_const_drams(nc)
    bc = _declare_bc_drams(nc, internal_otf=False)
    sch = Sched()
    ctx = ExitStack()
    with ctx:
        E = _mk_env(nc, sch, ctx)
        _load_consts(E, csrc)
        OTf = bc["OTf"]
        E.otf_own = lambda e: OTf[0:HPC * 128, :].rearrange("(hv p) t -> p hv t", p=128)
        E.otf_mate = lambda e: OTf[HPC * 128:, :].rearrange("(hv p) t -> p hv t", p=128)
        E.own_gate = []
        E.otf_eng = "y"
        E.gate = []
        for k in ("xin", "wo_sw", "WT_sw", "A_sw", "B_sw", "bias_sw", "probs_sw",
                  "outT", "attnP", "xres", "ffnX"):
            setattr(E, k, bc[k])
        _phaseBC(E)
        sch.emit(nc, ctx)
    return nc


def _sw(arr2d):
    K, M = arr2d.shape
    return np.ascontiguousarray(
        arr2d.reshape(K // 128, 128, M).transpose(1, 0, 2).reshape(128, -1))


_prog = None


def kernel(L0_attn_hidden_states, L0_attn_wq, L0_attn_wkv, L0_attn_wo,
           L0_ffn_W, L0_ffn_b, L0_ffn_A, L0_ffn_B, L0_ffn_probs):
    global _prog
    hs = np.ascontiguousarray(L0_attn_hidden_states, dtype=np.float32)
    wq = np.asarray(L0_attn_wq, dtype=np.float32)
    wkv = np.asarray(L0_attn_wkv, dtype=np.float32)
    wo = np.asarray(L0_attn_wo, dtype=np.float32)
    W = np.asarray(L0_ffn_W, dtype=np.float32)
    bia = np.asarray(L0_ffn_b, dtype=np.float32)
    A = np.asarray(L0_ffn_A, dtype=np.float32)
    Bw = np.asarray(L0_ffn_B, dtype=np.float32)
    probs = np.asarray(L0_ffn_probs, dtype=np.float32)

    ident = np.eye(128, dtype=BF)
    ones128 = np.ones((128, 128), dtype=BF)
    onesrow = np.ones((1, 128), dtype=BF)
    negrow = np.full((1, 128), -1.0, dtype=BF)
    ii, jj = np.indices((128, 128))
    mask = np.where(jj <= ii, 0.0, NEG).astype(BF)
    maskT = np.ascontiguousarray(mask.T)

    hsT = [np.ascontiguousarray(hs[b].T) for b in range(B)]
    wqkv_p = []
    for p in range(2):
        blocks = []
        for hl in range(HPC):
            g = p * HPC + hl
            blk = np.concatenate([
                _sw(wq[:, g * DQ: g * DQ + DN]),
                _sw(wkv[:, g * DN: (g + 1) * DN]),
                _sw(wkv[:, H * DN + g * DV: H * DN + (g + 1) * DV])], axis=1)
            blocks.append(blk)
        wqkv_p.append(np.ascontiguousarray(np.concatenate(blocks, axis=0)))

    wo_sw_p = []
    for p in range(2):
        order = list(range(p * HPC, (p + 1) * HPC)) + \
            list(range((1 - p) * HPC, (2 - p) * HPC))
        wo_perm = np.concatenate([wo[g * DV:(g + 1) * DV, :] for g in order], axis=0)
        wo_sw_p.append(np.ascontiguousarray(np.concatenate(
            [_sw(wo_perm[:, dc * 128:(dc + 1) * 128]) for dc in range(NKC)], axis=0)))
    WT_sw = np.ascontiguousarray(np.concatenate(
        [_sw(np.ascontiguousarray(W[oc * 128:(oc + 1) * 128, :].T)) for oc in range(NKC)], axis=0))
    A_sw = _sw(np.ascontiguousarray(A.transpose(2, 0, 1).reshape(D, 256)))
    B_sw = np.ascontiguousarray(
        Bw.transpose(0, 2, 1).reshape(256, D).reshape(2, 128, D)
        .transpose(1, 0, 2).reshape(128, 2 * D))
    bias_sw = np.ascontiguousarray(bia.reshape(NKC, 128).T)
    probs_sw = np.ascontiguousarray(np.repeat(probs, 16).reshape(2, 128).T)

    if _prog is None:
        _prog = build_fused()
    in_maps = []
    for c in range(8):
        b, p = c // 2, c % 2
        own = slice(p * NOWN, (p + 1) * NOWN)
        in_maps.append(dict(
            hsT=hsT[b], wqkv=wqkv_p[p],
            identA=ident, onesA=ones128, onesrowA=onesrow, negrowA=negrow,
            maskA=mask, maskTA=maskT,
            xin=np.ascontiguousarray(hsT[b][:, own]), wo_sw=wo_sw_p[p], WT_sw=WT_sw,
            A_sw=A_sw, B_sw=B_sw, bias_sw=bias_sw, probs_sw=probs_sw))
    res = run_bass_kernel_spmd(_prog, in_maps, core_ids=list(range(8)))

    out = np.empty((B, S, D), dtype=np.float32)
    for c in range(8):
        b, p = c // 2, c % 2
        out[b, p * NOWN:(p + 1) * NOWN, :] = res.results[c]["outT"].T
    return out


# version 23
# speedup vs baseline: 1.0104x; 1.0104x over previous
import numpy as np
import ml_dtypes
from contextlib import ExitStack
from types import SimpleNamespace

import concourse.bass as bass
import concourse.mybir as mybir
from concourse.bass_utils import run_bass_kernel_spmd

F32 = mybir.dt.float32
BF16 = mybir.dt.bfloat16
I32 = mybir.dt.int32
AF = mybir.ActivationFunctionType
ALU = mybir.AluOpType
AX = mybir.AxisListType
BF = ml_dtypes.bfloat16

DBG = dict(const_c=False, no_lsum=False, norm_tail_const=False,
           norm_no_reduce=False, norm_no_square=False,
           norm_no_rounds=False)

H, DN, DV, DQ = 32, 128, 128, 192
D = 4096
S = 2048
B = 4
HPC = 16
NOWN = 1024
NKC = D // 128
SCALE = float(np.float32(DQ) ** -0.5)
NEG = -1.0e30
EPS = 1e-5
GROUPS = [[0, 1], [2, 3], [4, 5], [6, 7]]


class Sem:
    __slots__ = ("name", "n", "h")

    def __init__(self, name):
        self.name, self.n, self.h = name, 0, None


class Sched:
    ENGS = ("t", "v", "s", "q", "y")

    def __init__(self):
        self.ops = {k: [] for k in self.ENGS}
        self.sems = []
        self.chain = {}

    def sem(self, name):
        s = Sem(name)
        self.sems.append(s)
        return s

    def op(self, eng, fn, waits=(), incs=(), chainable=True):
        ws = []
        for w in waits:
            if isinstance(w, tuple):
                ws.append((w[0], w[1]))
            else:
                ws.append((w, w.n))
        cs = self.chain.get(eng) if chainable else None
        if cs is not None and cs.n > 0:
            ws.append((cs, cs.n))
        ics = []
        for s, k in incs:
            s.n += k
            ics.append((s, k))
        if cs is not None and not any(s is cs for s, _ in ics):
            cs.n += 1
            ics.append((cs, 1))
        self.ops[eng].append((fn, ws, ics))

    def emit(self, nc, ctx):
        for s in self.sems:
            s.h = ctx.enter_context(nc.semaphore(s.name))
        block = ctx.enter_context(nc.Block())
        deco = dict(t=block.tensor, v=block.vector, s=block.scalar,
                    q=block.gpsimd, y=block.sync)

        def runner(ops):
            def run(eng):
                waited = {}
                for fn, ws, ics in ops:
                    for s, v in ws:
                        if v > 0 and waited.get(s, 0) < v:
                            eng.wait_ge(s.h, v)
                            waited[s] = v
                    inst = fn(eng)
                    for s, k in ics:
                        inst = inst.then_inc(s.h, k)
            return run
        for key in self.ENGS:
            deco[key](runner(list(self.ops[key])))


class Lanes:

    def __init__(self, sch, name, n):
        self.sch = sch
        self.sems = [sch.sem(f"{name}{i}") for i in range(n)]
        self.n = n
        self.i = 0

    def dma(self, eng, fn, waits=()):
        s = self.sems[self.i % self.n]
        self.i += 1
        w = list(waits)
        if s.n > 0:
            w.append((s, s.n))
        self.sch.op(eng, fn, waits=w, incs=[(s, 16)], chainable=False)
        return (s, s.n)

    def totals(self):
        return [(s, s.n) for s in self.sems if s.n > 0]


def call(name, *args, **kw):
    return lambda e, n=name, a=args, k=kw: getattr(e, n)(*a, **k)


def _phaseA(E, actx):
    sch, nc = E.sch, E.nc
    sT, sV, sS, sQ = E.sT, E.sV, E.sS, E.sQ
    yL, qL, oL = E.yL, E.qL, E.oL
    ident, ones, onesrow, mask, maskT = E.ident, E.ones, E.onesrow, E.mask, E.maskT
    negrow = E.negrow
    hsT, wqkv = E.hsT, E.wqkv
    consts = E.consts

    xb = actx.enter_context(nc.sbuf_tensor("xb", [128, NKC * S], BF16))
    wq_sb = actx.enter_context(nc.sbuf_tensor("wq_sb", [128, NKC * 128], BF16))
    wk_sb = actx.enter_context(nc.sbuf_tensor("wk_sb", [128, NKC * 128], BF16))
    wv_sb = actx.enter_context(nc.sbuf_tensor("wv_sb", [128, NKC * 128], BF16))
    wcol = (wq_sb, wk_sb, wv_sb)
    cdma = {}

    def dma_col(h, col, waits=()):
        cdma[(h, col)] = qL.dma("q", call(
            "dma_start", out=wcol[col][:],
            in_=wqkv[h * 128:(h + 1) * 128, col * (NKC * 128):(col + 1) * (NKC * 128)]),
            waits=waits)

    for c in range(3):
        dma_col(0, c)

    with (nc.sbuf_tensor("xstage", [128, 2 * S], F32) as xstage,
          nc.sbuf_tensor("xsq", [128, 2 * S], BF16) as xsq,
          nc.sbuf_tensor("m_sb", [128, S], F32) as m_sb,
          nc.sbuf_tensor("e2_sb", [128, S], F32) as e2_sb,
          nc.sbuf_tensor("r_sb", [128, S], F32) as r_sb,
          nc.psum_tensor("psSum", [128, S], F32) as psSum,
          nc.psum_tensor("psSq", [128, S], F32) as psSq):
        vdone, sdone, tdone = {}, {}, {}
        slots = [xstage[:, 0:S], xstage[:, S:2 * S], r_sb[:]]
        for kc in range(NKC):
            pg = kc % 2
            stg = slots[kc % 3]
            waits = []
            if kc >= 3:
                waits = [(sV, vdone[kc - 3]), (sS, sdone[kc - 3])]
            xeng = "y" if kc % 2 == 0 else "s"
            xlane = yL if kc % 2 == 0 else E.sL
            tok = xlane.dma(xeng, call("dma_start", out=stg,
                                       in_=hsT[kc * 128:(kc + 1) * 128, :]), waits=waits)
            sch.op("v", call("tensor_copy", xb[:, kc * S:(kc + 1) * S], stg),
                   waits=[tok], incs=[(sV, 1)])
            vdone[kc] = sV.n
            swaits = [tok] if kc < 2 else [tok, (sT, tdone[kc - 2])]
            sch.op("s", call("activation", xsq[:, pg * S:(pg + 1) * S],
                             stg, AF.Square),
                   waits=swaits, incs=[(sS, 1)])
            sdone[kc] = sS.n
            for tc in range(4):
                sch.op("t", call("matmul", psSum[:, tc * 512:(tc + 1) * 512],
                                 ones[:], xb[:, kc * S + tc * 512: kc * S + (tc + 1) * 512],
                                 start=(kc == 0), stop=(kc == NKC - 1)),
                       waits=[sV, sS, consts] if tc == 0 else [])
            for tc in range(4):
                sch.op("t", call("matmul", psSq[:, tc * 512:(tc + 1) * 512],
                                 ones[:], xsq[:, pg * S + tc * 512: pg * S + (tc + 1) * 512],
                                 start=(kc == 0), stop=(kc == NKC - 1)),
                       incs=[(sT, 1)] if tc == 3 else [])
            tdone[kc] = sT.n
        sch.op("v", call("tensor_scalar", m_sb[:], psSum[:], 1.0 / D, None, ALU.mult),
               waits=[(sT, sT.n)], incs=[(sV, 1)])
        sch.op("v", call("tensor_scalar", e2_sb[:], psSq[:], 1.0 / D, None, ALU.mult),
               incs=[(sV, 1)])
        sch.op("s", call("activation", r_sb[:], m_sb[:], AF.Square),
               waits=[sV], incs=[(sS, 1)])
        sch.op("v", call("tensor_tensor", e2_sb[:], e2_sb[:], r_sb[:], ALU.subtract),
               waits=[sS])
        sch.op("v", call("tensor_scalar", e2_sb[:], e2_sb[:], EPS, None, ALU.add),
               incs=[(sV, 1)])
        sch.op("s", call("activation", r_sb[:], e2_sb[:], AF.Ln),
               waits=[sV], incs=[(sS, 1)])
        sch.op("s", call("activation", r_sb[:], r_sb[:], AF.Exp, scale=-0.5),
               incs=[(sS, 1)])
        m16 = xsq[:, 0:S]
        r16 = xsq[:, S:2 * S]
        sch.op("v", call("tensor_copy", m16, m_sb[:]), waits=[(sT, tdone[NKC - 1])])
        sch.op("v", call("tensor_copy", r16, r_sb[:]), waits=[sS], incs=[(sV, 1)])
        mr_tok = sV.n
        vnormk = {}
        for kc in range(NKC):
            eng = "v" if kc % 2 == 0 else "q"
            w0 = [] if eng == "v" else [(sV, mr_tok), (sT, tdone[kc])]
            sch.op(eng, call("tensor_tensor", xb[:, kc * S:(kc + 1) * S],
                             xb[:, kc * S:(kc + 1) * S], m16, ALU.subtract),
                   waits=w0)
            sch.op(eng, call("tensor_tensor", xb[:, kc * S:(kc + 1) * S],
                             xb[:, kc * S:(kc + 1) * S], r16, ALU.mult),
                   incs=[(sV, 1)] if eng == "v" else [(E.sQ, 1)])
            vnormk[kc] = (sV if eng == "v" else E.sQ, (sV if eng == "v" else E.sQ).n)
        E.lnvnorm = sV.n
        E.lnqnorm = E.sQ.n
        E.vnormk = vnormk

    qT2 = actx.enter_context(nc.sbuf_tensor("qT2", [128, 2 * S], BF16))
    kT2 = actx.enter_context(nc.sbuf_tensor("kT2", [128, 2 * S], BF16))
    vT = actx.enter_context(nc.sbuf_tensor("vT", [128, S], BF16))
    Vtok = actx.enter_context(nc.sbuf_tensor("Vtok", [128, S], BF16))
    PT = actx.enter_context(nc.sbuf_tensor("PT", [128, 16 * 512], BF16))
    OTb = actx.enter_context(nc.sbuf_tensor("OTb", [128, 2 * S], BF16))
    qsq = actx.enter_context(nc.sbuf_tensor("qsq", [128, S // 2], BF16))
    nr32 = actx.enter_context(nc.sbuf_tensor("nr32", [128, 16], F32))
    cscr = actx.enter_context(nc.sbuf_tensor("cscr", [1, 8], F32))
    nCb = actx.enter_context(nc.sbuf_tensor("nCb", [128, 2], F32))
    cs16 = actx.enter_context(nc.sbuf_tensor("cs16", [1, 2], BF16))
    linvb = actx.enter_context(nc.sbuf_tensor("linvb", [128, 512], BF16))
    lrow32 = actx.enter_context(nc.sbuf_tensor("lrow32", [1, 512], F32))
    lrow16 = actx.enter_context(nc.sbuf_tensor("lrow16", [1, 512], BF16))

    psP = actx.enter_context(nc.psum_tensor("psP", [128, 1024], F32))
    psS2a = actx.enter_context(nc.psum_tensor("psS2a", [128, 512], F32))
    psS2b = actx.enter_context(nc.psum_tensor("psS2b", [128, 512], F32))
    psOa = actx.enter_context(nc.psum_tensor("psOa", [128, 512], F32))
    psOb = actx.enter_context(nc.psum_tensor("psOb", [128, 512], F32))
    psN = actx.enter_context(nc.psum_tensor("psN", [128, 512], F32))
    psLB = actx.enter_context(nc.psum_tensor("psLB", [128, 512], F32))

    st = SimpleNamespace(evac=0, avdone=0, ot_tok={}, trtok={},
                         projtok={}, tr_last=[], wq=None,
                         avls={}, expbank={0: 0, 1: 0}, otmult={0: 0, 1: 0},
                         linv_evac=0, ncb_tok={0: 0, 1: 0},
                         ncb_war={0: 0, 1: 0}, psn_war=0, psn_bc_war=0,
                         cscr_war=0, qsq_war=0, cscr_lnexp=0, normgens=[])

    def emit_proj_mms(h, part):
        col, tp = divmod(part, 2)
        buf = wcol[col]
        for kc in range(NKC):
            for th in range(2):
                tc = tp * 2 + th
                w0 = []
                if kc == 0 and th == 0:
                    w0 = [cdma[(h, col)], (sS, st.evac)]
                    if not (h == 0 and part == 0):
                        w0 += [(sV, E.lnvnorm), (E.sQ, E.lnqnorm)]
                if h == 0 and part == 0 and th == 0:
                    w0.append(E.vnormk[kc])
                sch.op("t", call("matmul", psP[:, th * 512:(th + 1) * 512],
                                 buf[:, kc * 128:(kc + 1) * 128],
                                 xb[:, kc * S + tc * 512: kc * S + (tc + 1) * 512],
                                 start=(kc == 0), stop=(kc == NKC - 1)),
                       waits=w0,
                       incs=[(sT, 1)] if kc == NKC - 1 and th == 1 else [])
            if kc % 8 == 7 and kc != NKC - 1:
                yield
        st.projtok[(h, part)] = sT.n

    def emit_proj_part(h, part):
        for _ in emit_proj_mms(h, part):
            pass
        finish_proj_part(h, part)

    def finish_proj_part(h, part):
        col, tp = divmod(part, 2)
        hp = h % 2
        if col == 0:
            dst = qT2[:, hp * S + tp * 1024: hp * S + (tp + 1) * 1024]
            sch.op("s", call("mul", dst, psP[:], SCALE), waits=[sT], incs=[(sS, 1)])
        elif col == 1:
            dst = kT2[:, hp * S + tp * 1024: hp * S + (tp + 1) * 1024]
            sch.op("s", call("copy", dst, psP[:]), waits=[sT], incs=[(sS, 1)])
        else:
            dst = vT[:, tp * 1024:(tp + 1) * 1024]
            sch.op("s", call("copy", dst, psP[:]),
                   waits=[sT] + list(st.tr_last), incs=[(sS, 1)])
        st.evac = sS.n
    def emit_vtrans(h):
        toks = {}
        for kt in range(16):
            w = [(sS, st.evac)] if kt == 0 else []
            if kt == 0 and st.avdone:
                w.append((sT, st.avdone))
            toks[kt] = yL.dma("y", call(
                "dma_start_transpose", Vtok[:, kt * 128:(kt + 1) * 128],
                vT[:, kt * 128:(kt + 1) * 128]), waits=w)
        st.trtok[h] = toks
        st.tr_last = [toks[kt] for kt in (12, 13, 14, 15)]

    onescol = E.ones[:, 0:1]

    def norm_item(h, qk_tok):
        hp = (h % 2) * S
        hc = h % 2
        sq_pe = 0
        first = True
        for half, src in ((0, qT2), (1, kT2)):
            for piece in range(2):
                sqtok = 0
                if not DBG["norm_no_square"]:
                    if first:
                        sqw = [qk_tok]
                        if st.qsq_war:
                            sqw.append((sT, st.qsq_war))
                        first = False
                    else:
                        sqw = [(sT, sq_pe)]
                    sch.op("s", call("activation", qsq[:],
                                     src[:, hp + piece * 1024: hp + (piece + 1) * 1024],
                                     AF.Square), waits=sqw, incs=[(sS, 1)])
                    sqtok = sS.n
                    yield
                if DBG["norm_no_rounds"]:
                    continue
                for c in range(2):
                    w = []
                    if c == 0:
                        if sqtok:
                            w.append((sS, sqtok))
                        if st.psn_war:
                            w.append((sV, st.psn_war))
                        if st.psn_bc_war:
                            w.append((sV, st.psn_bc_war))
                    sch.op("t", call("matmul", psN[:, 0:512], ones[:],
                                     qsq[:, c * 512:(c + 1) * 512],
                                     start=(c == 0), stop=(c == 1)),
                           waits=w, incs=[(sT, 1)] if c == 1 else [])
                sq_pe = sT.n
                nri = half * 2 + piece
                if not DBG["norm_no_reduce"]:
                    sch.op("v", call("tensor_reduce", nr32[:, nri:nri + 1],
                                     psN[:, 0:512], AX.X, ALU.max,
                                     negate=True),
                           waits=[(sT, sq_pe)], incs=[(sV, 1)])
                    st.psn_war = sV.n
                yield
        st.qsq_war = sq_pe
        if DBG["norm_tail_const"]:
            ncw2 = [(sS, st.ncb_war[hc])] if st.ncb_war[hc] else []
            sch.op("v", call("memset", nCb[:, hc:hc + 1], -30.0),
                   waits=ncw2, incs=[(sV, 1)])
            st.ncb_tok[hc] = sV.n
            return
        sch.op("v", call("tensor_reduce", cscr[0:1, 0:1], nr32[0:1, 0:2],
                         AX.X, ALU.min),
               waits=[(sS, st.cscr_lnexp)] if st.cscr_lnexp else [])
        sch.op("v", call("tensor_reduce", cscr[0:1, 1:2], nr32[0:1, 2:4],
                         AX.X, ALU.min))
        sch.op("v", call("tensor_tensor", cscr[0:1, 2:3], cscr[0:1, 0:1],
                         cscr[0:1, 1:2], ALU.mult), incs=[(sV, 1)])
        prod_tok = sV.n
        cw = [(sV, prod_tok)]
        if st.cscr_war:
            cw.append((sT, st.cscr_war))
        sch.op("s", call("activation", cscr[0:1, 3:4], cscr[0:1, 2:3], AF.Ln),
               waits=cw, incs=[(sS, 1)])
        sch.op("s", call("activation", cscr[0:1, 4:5], cscr[0:1, 3:4], AF.Exp,
                         scale=0.5), incs=[(sS, 1)])
        c_tok = sS.n
        st.cscr_lnexp = c_tok
        yield
        cw2 = [(sS, c_tok)]
        if st.cscr_war:
            cw2.append((sT, st.cscr_war))
        sch.op("v", call("tensor_copy", cs16[0:1, 0:1], cscr[0:1, 4:5]),
               waits=cw2, incs=[(sV, 1)])
        bw = [(sV, sV.n)]
        if st.psn_bc_war:
            bw.append((sV, st.psn_bc_war))
        sch.op("t", call("matmul", psN[:, 1:2], negrow[:], cs16[0:1, 0:1],
                         start=True, stop=True), waits=bw, incs=[(sT, 1)])
        st.cscr_war = sT.n
        ncw = [(sT, st.cscr_war)]
        if st.ncb_war[hc]:
            ncw.append((sS, st.ncb_war[hc]))
        sch.op("v", call("tensor_copy", nCb[:, hc:hc + 1], psN[:, 1:2]),
               waits=ncw, incs=[(sV, 1)])
        st.ncb_tok[hc] = sV.n
        st.psn_bc_war = sV.n

    def emit_head_attn(h, qkv_ready):
        hp = (h % 2) * S
        hc = h % 2
        exp_tok = {}
        lsum_tok = {}

        def finalize(g):
            psO = psOa if g % 2 == 0 else psOb
            if DBG["no_lsum"]:
                ob = hc * S
                otw = [(sT, lsum_tok[g])]
                if g == 0 and h >= 2 and (h - 2) in st.ot_tok:
                    otw.append(st.ot_tok[h - 2])
                sch.op("v", call("tensor_copy",
                                 OTb[:, ob + g * 512: ob + (g + 1) * 512],
                                 psO[:, 0:512]), waits=otw, incs=[(sV, 1)])
                st.otmult[g % 2] = sV.n
                return
            sch.op("v", call("reciprocal", lrow32[:], psLB[0:1, 0:512]),
                   waits=[(sT, lsum_tok[g])])
            sch.op("v", call("tensor_copy", lrow16[:], lrow32[:]), incs=[(sV, 1)])
            sch.op("t", call("matmul", psLB[:, 0:512], onesrow[:], lrow16[0:1, :],
                             start=True, stop=True), waits=[(sV, sV.n)],
                   incs=[(sT, 1)])
            sch.op("v", call("tensor_copy", linvb[:], psLB[:]),
                   waits=[(sT, sT.n)], incs=[(sV, 1)])
            st.linv_evac = sV.n
            ob = hc * S
            otw = []
            if g == 0 and h >= 2 and (h - 2) in st.ot_tok:
                otw = [st.ot_tok[h - 2]]
            sch.op("v", call("tensor_tensor", OTb[:, ob + g * 512: ob + (g + 1) * 512],
                             psO[:, 0:512], linvb[:], ALU.mult),
                   waits=otw, incs=[(sV, 1)])
            st.otmult[g % 2] = sV.n

        for g in range(4):
            for kt in range(4 * g + 4):
                ps2 = psS2a if kt % 2 == 0 else psS2b
                off = max(0, (kt - 4 * g)) * 128
                w = []
                if g == 0 and kt == 0:
                    w = [(sS, qkv_ready), (sV, st.ncb_tok[hc])]
                if st.expbank[kt % 2]:
                    w.append((sS, st.expbank[kt % 2]))
                sch.op("t", call("matmul", ps2[:, off:512],
                                 kT2[:, hp + kt * 128: hp + (kt + 1) * 128],
                                 qT2[:, hp + g * 512 + off: hp + (g + 1) * 512],
                                 start=True, stop=True), waits=w, incs=[(sT, 1)])
                mm_tok = sT.n
                ew = [(sT, mm_tok)]
                if st.avls.get(kt):
                    ew.append((sT, st.avls[kt]))
                if g == 0 and kt == 0:
                    ew.append((sV, st.ncb_tok[hc]))
                if kt >= 4 * g:
                    sch.op("v", call("tensor_tensor", ps2[:, off:off + 128],
                                     ps2[:, off:off + 128], maskT[:], ALU.add),
                           waits=[(sT, mm_tok)], incs=[(sV, 1)])
                    ew.append((sV, sV.n))
                sch.op("s", call("activation", PT[:, kt * 512 + off: (kt + 1) * 512],
                                 ps2[:, off:512], AF.Exp, bias=nCb[:, hc:hc + 1]),
                       waits=ew, incs=[(sS, 1)])
                exp_tok[kt] = sS.n
                st.expbank[kt % 2] = sS.n
                st.wq.pump()
            if g > 0:
                finalize(g - 1)
            psO = psOa if g % 2 == 0 else psOb
            for kt in range(4 * g + 4):
                off = max(0, (kt - 4 * g)) * 128
                w = [(sS, exp_tok[kt])]
                if kt == 0 and st.otmult[g % 2]:
                    w.append((sV, st.otmult[g % 2]))
                tk = st.trtok[h].pop(kt, None)
                if tk is not None:
                    w.append(tk)
                sch.op("t", call("matmul", psO[:, off:512],
                                 Vtok[:, kt * 128:(kt + 1) * 128],
                                 PT[:, kt * 512 + off: (kt + 1) * 512],
                                 start=(kt == 0), stop=(kt == 4 * g + 3)),
                       waits=w, incs=[(sT, 1)])
                if not DBG["no_lsum"]:
                    lw = []
                    if kt == 0 and st.linv_evac:
                        lw.append((sV, st.linv_evac))
                    sch.op("t", call("matmul", psLB[0:1, off:512], onescol,
                                     PT[:, kt * 512 + off: (kt + 1) * 512],
                                     start=(kt == 0), stop=(kt == 4 * g + 3)),
                           waits=lw, incs=[(sT, 1)])
                st.avls[kt] = sT.n
                if kt % 2 == 1:
                    st.wq.pump()
            lsum_tok[g] = sT.n
            st.avdone = sT.n
            st.wq.pump()
        st.wq.pump()
        finalize(3)
        st.ncb_war[hc] = sS.n
        st.ot_tok[h] = E.ot_store(h, OTb[:, hc * S:(hc + 1) * S], [(sV, sV.n)])

    def proj_item(h):
        for part in range(6):
            for _ in emit_proj_mms(h, part):
                yield
            finish_proj_part(h, part)
            if part == 3 and not DBG["const_c"]:
                wq.push("norm", norm_item(h, (sS, st.evac)))
            if part in (1, 3, 5) and h + 1 < HPC:
                dma_col(h + 1, part // 2, waits=[(sT, st.projtok[(h, part)])])
            hook = getattr(E, "proj_hook", None)
            if hook is not None:
                hook(h, part, (sT, st.projtok[(h, part)]))
            yield

    def vtrans_item(h):
        emit_vtrans(h)
        yield

    class WorkQueue:

        def __init__(self):
            self.items = []

        def push(self, kind, gen):
            self.items.append((kind, gen))

        def push_front(self, kind, gen):
            self.items.insert(0, (kind, gen))

        def pump(self):
            while self.items:
                kind, gen = self.items[0]
                try:
                    next(gen)
                    return
                except StopIteration:
                    self.items.pop(0)

        def drain_kind(self, kinds):
            while self.items and self.items[0][0] in kinds:
                kind, gen = self.items[0]
                try:
                    next(gen)
                except StopIteration:
                    self.items.pop(0)

        def drain_where(self, kind):
            for i in range(len(self.items) - 1, -1, -1):
                if self.items[i][0] == kind:
                    gen = self.items[i][1]
                    while True:
                        try:
                            next(gen)
                        except StopIteration:
                            break
                    self.items.pop(i)

        def drain(self):
            while self.items:
                self.pump()

    wq = WorkQueue()
    st.wq = wq

    for _ in proj_item(0):
        pass
    emit_vtrans(0)
    if DBG["const_c"]:
        sch.op("v", call("memset", nCb[:, 0:2], -30.0), incs=[(sV, 1)])
        st.ncb_tok[0] = st.ncb_tok[1] = sV.n

    for h in range(HPC):
        qkv_ready = st.evac
        wq.drain_where("norm")
        wq.drain_where("vtrans")
        if h + 1 < HPC:
            wq.push("proj", proj_item(h + 1))
        emit_head_attn(h, qkv_ready)
        wq.drain_where("proj")
        if h + 1 < HPC:
            wq.items.insert(min(1, len(wq.items)), ("vtrans", vtrans_item(h + 1)))
        else:
            wq.drain()
    E.projlast = (E.sT, st.projtok[(HPC - 1, 5)])


def _phaseBC(E):
    sch, nc, ctx = E.sch, E.nc, E.ctx
    sT, sV, sS, sQ = E.sT, E.sV, E.sS, E.sQ
    yL, pL, sL, qL, oF = E.yL, E.pL, E.sL, E.qL, E.oF
    ones = E.ones
    xin, wo_sw, WT_sw, A_sw, B_sw, bias_sw, probs_sw = (
        E.xin, E.wo_sw, E.WT_sw, E.A_sw, E.B_sw, E.bias_sw, E.probs_sw)
    outT, attnP, xres, ffnX = E.outT, E.attnP, E.xres, E.ffnX
    gate = list(E.gate)

    def lnfin(m_sl, r_sl, ps_sum, ps_sq):
        sch.op("v", call("tensor_scalar", m_sl, ps_sum[:], 1.0 / D, None, ALU.mult),
               waits=[(sT, sT.n)], incs=[(sV, 1)])
        sch.op("v", call("tensor_scalar", tmp[:], ps_sq[:], 1.0 / D, None, ALU.mult),
               incs=[(sV, 1)])
        sch.op("s", call("activation", r_sl, m_sl, AF.Square), waits=[sV], incs=[(sS, 1)])
        sch.op("v", call("tensor_tensor", tmp[:], tmp[:], r_sl, ALU.subtract),
               waits=[sS])
        sch.op("v", call("tensor_scalar", tmp[:], tmp[:], EPS, None, ALU.add),
               incs=[(sV, 1)])
        sch.op("s", call("activation", tmp[:], tmp[:], AF.Ln),
               waits=[sV], incs=[(sS, 1)])
        sch.op("s", call("activation", r_sl, tmp[:], AF.Exp, scale=-0.5),
               incs=[(sS, 1)])
        return (sS, sS.n)

    wbuf = getattr(E, "wbuf", None)
    if wbuf is None:
        wbuf = ctx.enter_context(nc.sbuf_tensor("wbuf", [128, 2 * D], BF16))
    aP = ctx.enter_context(nc.sbuf_tensor("aP", [128, 4 * NOWN], F32))
    x16 = ctx.enter_context(nc.sbuf_tensor("x16", [128, 2 * NOWN], BF16))
    sq16 = ctx.enter_context(nc.sbuf_tensor("sq16", [128, 2 * NOWN], BF16))
    xinb = ctx.enter_context(nc.sbuf_tensor("xinb", [128, 4 * NOWN], F32))
    xrb = ctx.enter_context(nc.sbuf_tensor("xrb", [128, 4 * NOWN], F32))
    m_a = ctx.enter_context(nc.sbuf_tensor("m_a", [128, NOWN], F32))
    r_a = ctx.enter_context(nc.sbuf_tensor("r_a", [128, NOWN], F32))
    m_2 = ctx.enter_context(nc.sbuf_tensor("m_2", [128, NOWN], F32))
    r_2 = ctx.enter_context(nc.sbuf_tensor("r_2", [128, NOWN], F32))
    m_3 = ctx.enter_context(nc.sbuf_tensor("m_3", [128, NOWN], F32))
    r_3 = ctx.enter_context(nc.sbuf_tensor("r_3", [128, NOWN], F32))
    tmp = aP[:, 0:NOWN]
    fb16 = ctx.enter_context(nc.sbuf_tensor("fb16", [128, 4 * NOWN], BF16))
    mr16 = fb16
    bias_sb = ctx.enter_context(nc.sbuf_tensor("bias_sb", [128, NKC], F32))
    probs_sb = ctx.enter_context(nc.sbuf_tensor("probs_sb", [128, 2], F32))

    psW = ctx.enter_context(nc.psum_tensor("psW", [128, 2048], F32))
    psSum = ctx.enter_context(nc.psum_tensor("psSumB", [128, 1024], F32))
    psSq = ctx.enter_context(nc.psum_tensor("psSqB", [128, 1024], F32))

    bscope = ExitStack()
    OTsb = bscope.enter_context(nc.sbuf_tensor("OTsb", [128, H * NOWN], BF16))

    otf_lane = yL if E.otf_eng == "y" else qL
    ot_own = otf_lane.dma(E.otf_eng, lambda e: e.dma_start(
        out=OTsb[:, 0:HPC * NOWN].rearrange("p (hv t) -> p hv t", hv=HPC),
        in_=E.otf_own(e)), waits=E.own_gate)
    ot_mate = otf_lane.dma(E.otf_eng, lambda e: e.dma_start(
        out=OTsb[:, HPC * NOWN:].rearrange("p (hv t) -> p hv t", hv=HPC),
        in_=E.otf_mate(e)), waits=gate)
    sch.op("y", call("dma_start", out=bias_sb[:], in_=bias_sw[:]),
           incs=[(E.dC, 16)])
    sch.op("y", call("dma_start", out=probs_sb[:], in_=probs_sw[:]),
           incs=[(E.dC, 16)])
    cn = (E.dC, E.dC.n)

    wtok = dict(getattr(E, "pre_wtok", {}))
    slot_last = {}
    sp_at, sp_xr, sp_ff = {}, {}, {}
    statT = {}
    fin_done = {}
    stat_q = []

    def emit_stats(ps_sum, ps_sq, ph, dc, pg, vtok, stok):
        if ph == "l2":
            xsrc = [xres16[:, dc * NOWN + tc * 512: dc * NOWN + (tc + 1) * 512]
                    for tc in range(2)]
        else:
            xsrc = [x16[:, pg * NOWN + tc * 512: pg * NOWN + (tc + 1) * 512]
                    for tc in range(2)]
        for tc in range(2):
            sch.op("t", call("matmul", ps_sum[:, tc * 512:(tc + 1) * 512], ones[:],
                             xsrc[tc], start=(dc == 0), stop=(dc == NKC - 1)),
                   waits=[vtok, stok] if tc == 0 else [])
            sch.op("t", call("matmul", ps_sq[:, tc * 512:(tc + 1) * 512], ones[:],
                             sq16[:, pg * NOWN + tc * 512: pg * NOWN + (tc + 1) * 512],
                             start=(dc == 0), stop=(dc == NKC - 1)),
                   incs=[(sT, 1)] if tc == 1 else [])
        statT[(ph, dc)] = sT.n

    for si0 in (0, 1):
        if si0 not in wtok:
            wtok[si0] = qL.dma("q", call(
                "dma_start", out=wbuf[:, si0 * D:(si0 + 1) * D],
                in_=wo_sw[si0 * 128:(si0 + 1) * 128, :]), waits=gate)
    for dc in range(NKC):
        pg = dc % 2
        if dc + 1 < NKC and (dc + 1) not in wtok:
            wtok[dc + 1] = qL.dma("q", call(
                "dma_start", out=wbuf[:, (1 - pg) * D:(2 - pg) * D],
                in_=wo_sw[(dc + 1) * 128:(dc + 2) * 128, :]), waits=[(sT, sT.n)])
        for hv in range(H):
            for tc in range(2):
                w0 = []
                if hv == 0 and tc == 0:
                    w0 = [wtok[dc], ot_own, cn]
                    lk = slot_last.pop(("psW", pg), None)
                    if lk is not None:
                        w0.extend(lk if isinstance(lk, list) else [lk])
                if hv == HPC and tc == 0:
                    w0 = [ot_mate]
                sch.op("t", call("matmul", psW[:, (2 * pg + tc) * 512:(2 * pg + tc + 1) * 512],
                                 wbuf[:, pg * D + hv * 128: pg * D + (hv + 1) * 128],
                                 OTsb[:, hv * NOWN + tc * 512: hv * NOWN + (tc + 1) * 512],
                                 start=(hv == 0), stop=(hv == H - 1)),
                       waits=w0, incs=[(sT, 1)] if hv == H - 1 and tc == 1 else [])
        sl = aP[:, (dc % 4) * NOWN:((dc % 4) + 1) * NOWN]
        ew = [sT]
        lk = slot_last.pop(("aP", dc % 4), None)
        if lk is not None:
            ew.extend(lk if isinstance(lk, list) else [lk])
        sch.op("v", call("tensor_copy", sl, psW[:, 2 * pg * 512: 2 * pg * 512 + 1024]),
               waits=ew, incs=[(sV, 1)])
        ev = (sV, sV.n)
        slot_last[("psW", pg)] = ev
        sw_ = [sV] if dc < 2 else [sV, (sT, statT[("wo", dc - 2)])]
        sch.op("s", call("activation", sq16[:, pg * NOWN:(pg + 1) * NOWN], sl,
                         AF.Square), waits=sw_, incs=[(sS, 1)])
        sqv = (sS, sS.n)
        xw16 = [] if dc < 2 else [(sT, statT[("wo", dc - 2)])]
        lkx = slot_last.pop(("x16", pg), None)
        if lkx is not None:
            xw16.append(lkx)
        sch.op("v", call("tensor_copy", x16[:, pg * NOWN:(pg + 1) * NOWN], sl),
               waits=xw16, incs=[(sV, 1)])
        xv = (sV, sV.n)
        stat_q.append(("wo", dc, pg, xv, sqv))
        if dc > 0:
            emit_stats(psSum, psSq, *stat_q.pop(0))
        sp_at[dc] = pL.dma("y", call("dma_start",
                                     out=attnP[dc * 128:(dc + 1) * 128, :],
                                     in_=x16[:, pg * NOWN:(pg + 1) * NOWN]),
                           waits=[xv])
        slot_last[("x16", pg)] = sp_at[dc]
        slot_last[("aP", dc % 4)] = [xv, sqv]
    emit_stats(psSum, psSq, *stat_q.pop(0))
    wo_last = (sT, sT.n)
    ma_tok = lnfin(m_a[:], r_a[:], psSum, psSq)
    bscope.close()
    xres16 = ctx.enter_context(nc.sbuf_tensor("xres16", [128, NKC * NOWN], BF16))
    A_sb = ctx.enter_context(nc.sbuf_tensor("A_sb", [128, NKC * 256], BF16))
    B_sb = ctx.enter_context(nc.sbuf_tensor("B_sb", [128, 2 * D], BF16))
    hs16 = ctx.enter_context(nc.sbuf_tensor("hs16", [128, 2 * NOWN], BF16))
    atok = qL.dma("q", call("dma_start", out=A_sb[:], in_=A_sw[:]), waits=[wo_last])
    btok = qL.dma("q", call("dma_start", out=B_sb[:], in_=B_sw[:]), waits=[wo_last])
    wtok[NKC] = qL.dma("q", call("dma_start", out=wbuf[:, 0:D],
                                 in_=WT_sw[0:128, :]), waits=[wo_last])
    wtok[NKC + 1] = qL.dma("q", call("dma_start", out=wbuf[:, D:2 * D],
                                     in_=WT_sw[128:256, :]), waits=[wo_last])

    resid_tok = {}
    rload = {}

    def resid_load(dc):
        s4 = dc % 4
        w = [sp_at[dc]]
        lk = slot_last.pop(("fb16", s4), None)
        if lk is not None:
            w.extend(lk if isinstance(lk, list) else [lk])
        atk = sL.dma("s", call("dma_start", out=fb16[:, s4 * NOWN:(s4 + 1) * NOWN],
                               in_=attnP[dc * 128:(dc + 1) * 128, :]), waits=w)
        xw = []
        lk = slot_last.pop(("xinb", s4), None)
        if lk is not None:
            xw.extend(lk if isinstance(lk, list) else [lk])
        xtk = yL.dma("y", call("dma_start", out=xinb[:, s4 * NOWN:(s4 + 1) * NOWN],
                               in_=xin[dc * 128:(dc + 1) * 128, :]), waits=xw)
        rload[dc] = (atk, xtk)

    def resid_chunk_ops(dc):
        s4 = dc % 4
        sl = aP[:, s4 * NOWN:(s4 + 1) * NOWN]
        atk, xtk = rload.pop(dc)
        xr = xrb[:, s4 * NOWN:(s4 + 1) * NOWN]
        lk = slot_last.pop(("xrb", s4), None)
        lk = [] if lk is None else (lk if isinstance(lk, list) else [lk])
        lka = slot_last.pop(("aP", s4), None)
        lka = [] if lka is None else (lka if isinstance(lka, list) else [lka])
        steps = []
        for eng, c0, c1 in (("v", 0, 640), ("q", 640, 1024)):
            w0 = [atk, ma_tok] + lka + ([wo_last] if dc == 0 and eng == "q" else [])
            steps.append((eng, call("tensor_tensor", sl[:, c0:c1],
                                    fb16[:, s4 * NOWN + c0: s4 * NOWN + c1],
                                    m_a[:, c0:c1], ALU.subtract), w0, []))
            steps.append((eng, call("tensor_tensor", sl[:, c0:c1], sl[:, c0:c1],
                                    r_a[:, c0:c1], ALU.mult), [], []))
            w1 = [xtk] + lk
            steps.append((eng, call("tensor_tensor", xr[:, c0:c1], sl[:, c0:c1],
                                    xinb[:, s4 * NOWN + c0: s4 * NOWN + c1],
                                    ALU.add), w1, []))
            steps.append((eng, call("tensor_copy",
                                    xres16[:, dc * NOWN + c0: dc * NOWN + c1],
                                    xr[:, c0:c1]), [], ["tok"]))
        return steps

    def emit_pairwise(step_lists, fini):
        idx = [0] * len(step_lists)
        toks = [[] for _ in step_lists]
        done = 0
        while done < len(step_lists):
            done = 0
            for ci, steps in enumerate(step_lists):
                if idx[ci] >= len(steps):
                    done += 1
                    continue
                eng, fn, ws, mark = steps[idx[ci]]
                sem = sV if eng == "v" else sQ
                sch.op(eng, fn, waits=ws,
                       incs=[(sem, 1)] if mark else [])
                if mark:
                    toks[ci].append((sem, sem.n))
                idx[ci] += 1
        for ci, f in enumerate(fini):
            f(toks[ci])

    def resid_fini(dc):
        s4 = dc % 4
        xr = xrb[:, s4 * NOWN:(s4 + 1) * NOWN]

        def f(toks):
            resid_tok[dc] = toks
            sp_xr[dc] = E.vL.dma("q", call(
                "dma_start", out=xres[dc * 128:(dc + 1) * 128, :], in_=xr),
                waits=toks)
            slot_last[("xrb", s4)] = sp_xr[dc]
            slot_last[("xinb", s4)] = list(toks)
            slot_last[("fb16", s4)] = list(toks)
            slot_last[("aP", s4)] = list(toks)
            sw_ = list(toks) + ([] if dc < 2 else [(sT, statT[("l2", dc - 2)])])
            sch.op("s", call("activation",
                             sq16[:, (dc % 2) * NOWN:((dc % 2) + 1) * NOWN],
                             xr, AF.Square), waits=sw_, incs=[(sS, 1)])
            stat_q.append(("l2", dc, dc % 2, toks[-1], (sS, sS.n)))
        return f

    def l2_stats(d2):
        ph, _, pg2, vt, st_ = stat_q.pop(0)
        rt = resid_tok[d2]
        sch.op("t", call("matmul", psSum[:, 0:512], ones[:],
                         xres16[:, d2 * NOWN: d2 * NOWN + 512],
                         start=(d2 == 0), stop=(d2 == NKC - 1)),
               waits=[rt[0], rt[1], st_])
        sch.op("t", call("matmul", psSum[:, 512:1024], ones[:],
                         xres16[:, d2 * NOWN + 512: d2 * NOWN + 1024],
                         start=(d2 == 0), stop=(d2 == NKC - 1)))
        sch.op("t", call("matmul", psSq[:, 0:512], ones[:],
                         sq16[:, pg2 * NOWN: pg2 * NOWN + 512],
                         start=(d2 == 0), stop=(d2 == NKC - 1)))
        sch.op("t", call("matmul", psSq[:, 512:1024], ones[:],
                         sq16[:, pg2 * NOWN + 512: pg2 * NOWN + 1024],
                         start=(d2 == 0), stop=(d2 == NKC - 1)),
               incs=[(sT, 1)])
        statT[("l2", d2)] = sT.n

    for dpre in range(3):
        resid_load(dpre)
    for dc in range(0, NKC, 2):
        while stat_q and stat_q[0][1] <= dc - 1:
            l2_stats(stat_q[0][1])
        emit_pairwise([resid_chunk_ops(dc), resid_chunk_ops(dc + 1)],
                      [resid_fini(dc), resid_fini(dc + 1)])
        for dn in (dc + 3, dc + 4):
            if dn < NKC:
                resid_load(dn)
    while stat_q:
        l2_stats(stat_q[0][1])
    m2_tok = lnfin(m_2[:], r_2[:], psSum, psSq)
    sch.op("v", call("tensor_copy", mr16[:, 0:NOWN], m_2[:]), waits=[m2_tok])
    sch.op("v", call("tensor_copy", mr16[:, NOWN:2 * NOWN], r_2[:]), incs=[(sV, 1)])
    mr_tok = (sV, sV.n)

    ln2t_tok = {}
    for dc in range(NKC):
        toks = []
        for eng, c0, c1 in (("v", 0, 640), ("q", 640, 1024)):
            xs = xres16[:, dc * NOWN + c0: dc * NOWN + c1]
            sch.op(eng, call("tensor_tensor", xs, xs, mr16[:, c0:c1], ALU.subtract),
                   waits=[mr_tok, (sT, statT[("l2", NKC - 1)])])
            sch.op(eng, call("tensor_tensor", xs, xs, mr16[:, NOWN + c0: NOWN + c1],
                             ALU.mult),
                   incs=[(sV, 1)] if eng == "v" else [(sQ, 1)])
            toks.append((sV, sV.n) if eng == "v" else (sQ, sQ.n))
        ln2t_tok[dc] = toks
    LN2T = xres16
    for dc in range(NKC):
        for erc in range(2):
            for tc in range(2):
                w0 = []
                if tc == 0 and erc == 0:
                    w0 = list(ln2t_tok[dc]) + ([atok, btok, cn] if dc == 0 else [])
                    if dc == 0:
                        for kk in (("psW", 0), ("psW", 1)):
                            lk = slot_last.pop(kk, None)
                            if lk is not None:
                                w0.extend(lk if isinstance(lk, list) else [lk])
                sch.op("t", call("matmul",
                                 psW[:, (2 * erc + tc) * 512:(2 * erc + tc + 1) * 512],
                                 A_sb[:, dc * 256 + erc * 128: dc * 256 + (erc + 1) * 128],
                                 LN2T[:, dc * NOWN + tc * 512: dc * NOWN + (tc + 1) * 512],
                                 start=(dc == 0), stop=(dc == NKC - 1)),
                       waits=w0,
                       incs=[(sT, 1)] if dc == NKC - 1 and erc == 1 and tc == 1 else [])
    for erc in range(2):
        sch.op("v", call("tensor_scalar", hs16[:, erc * NOWN:(erc + 1) * NOWN],
                         psW[:, 2 * erc * 512: 2 * erc * 512 + 1024],
                         probs_sb[:, erc:erc + 1], None, ALU.mult),
               waits=[sT], incs=[(sV, 1)])
    hs_done = (sV, sV.n)
    slot_last[("psW", 0)] = hs_done
    slot_last[("psW", 1)] = hs_done

    for oc in range(NKC):
        pg = oc % 2
        sio = NKC + oc
        if oc + 1 < NKC and (sio + 1) not in wtok:
            wtok[sio + 1] = qL.dma("q", call(
                "dma_start", out=wbuf[:, (1 - pg) * D:(2 - pg) * D],
                in_=WT_sw[(oc + 1) * 128:(oc + 2) * 128, :]), waits=[(sT, sT.n)])
        for dc in range(NKC):
            for tc in range(2):
                w0 = []
                if dc == 0 and tc == 0:
                    w0 = [wtok[sio], hs_done]
                    lk = slot_last.pop(("psW", pg), None)
                    if lk is not None:
                        w0.extend(lk if isinstance(lk, list) else [lk])
                sch.op("t", call("matmul",
                                 psW[:, (2 * pg + tc) * 512:(2 * pg + tc + 1) * 512],
                                 wbuf[:, pg * D + dc * 128: pg * D + (dc + 1) * 128],
                                 LN2T[:, dc * NOWN + tc * 512: dc * NOWN + (tc + 1) * 512],
                                 start=(dc == 0), stop=False), waits=w0)
        for erc in range(2):
            for tc in range(2):
                sch.op("t", call("matmul",
                                 psW[:, (2 * pg + tc) * 512:(2 * pg + tc + 1) * 512],
                                 B_sb[:, erc * D + oc * 128: erc * D + (oc + 1) * 128],
                                 hs16[:, erc * NOWN + tc * 512: erc * NOWN + (tc + 1) * 512],
                                 start=False, stop=(erc == 1)),
                       incs=[(sT, 1)] if erc == 1 and tc == 1 else [])
        s4 = oc % 4
        sl = aP[:, s4 * NOWN:(s4 + 1) * NOWN]
        w = [sT]
        lk = slot_last.pop(("aP", s4), None)
        if lk is not None:
            w.extend(lk if isinstance(lk, list) else [lk])
        if (oc - 2) in sp_ff:
            w.append(sp_ff[oc - 2])
        sch.op("v", call("tensor_scalar", sl, psW[:, 2 * pg * 512: 2 * pg * 512 + 1024],
                         bias_sb[:, oc:oc + 1], None, ALU.add),
               waits=w, incs=[(sV, 1)])
        ev = (sV, sV.n)
        slot_last[("psW", pg)] = ev
        sw_ = [sV] if oc < 2 else [sV, (sT, statT[("l3", oc - 2)])]
        sch.op("s", call("activation", sq16[:, pg * NOWN:(pg + 1) * NOWN], sl,
                         AF.Square), waits=sw_, incs=[(sS, 1)])
        sqv = (sS, sS.n)
        xw16 = [] if oc < 2 else [(sT, statT[("l3", oc - 2)])]
        lkx = slot_last.pop(("x16", pg), None)
        if lkx is not None:
            xw16.append(lkx)
        sch.op("v", call("tensor_copy", x16[:, pg * NOWN:(pg + 1) * NOWN], sl),
               waits=xw16, incs=[(sV, 1)])
        xv = (sV, sV.n)
        stat_q.append(("l3", oc, pg, xv, sqv))
        if oc > 0:
            emit_stats(psSum, psSq, *stat_q.pop(0))
        sp_ff[oc] = pL.dma("y", call("dma_start",
                                     out=ffnX[oc * 128:(oc + 1) * 128, :],
                                     in_=x16[:, pg * NOWN:(pg + 1) * NOWN]),
                           waits=[xv])
        slot_last[("x16", pg)] = sp_ff[oc]
        slot_last[("aP", s4)] = [xv, sqv]
    emit_stats(psSum, psSq, *stat_q.pop(0))
    m3_tok = lnfin(m_3[:], r_3[:], psSum, psSq)
    xw0 = [m3_tok]
    for lkx in (slot_last.pop(("x16", 0), None), slot_last.pop(("x16", 1), None)):
        if lkx is not None:
            xw0.append(lkx)
    sch.op("v", call("tensor_copy", x16[:, 0:NOWN], m_3[:]), waits=xw0)
    sch.op("v", call("tensor_copy", x16[:, NOWN:2 * NOWN], r_3[:]), incs=[(sV, 1)])
    m316_tok = (sV, sV.n)

    fload = {}

    def fin_load(dc):
        s4 = dc % 4
        w = [sp_ff[dc]]
        lk = slot_last.pop(("fb16", s4), None)
        if lk is not None:
            w.extend(lk if isinstance(lk, list) else [lk])
        ftk = E.vL.dma("q", call("dma_start", out=fb16[:, s4 * NOWN:(s4 + 1) * NOWN],
                                 in_=ffnX[dc * 128:(dc + 1) * 128, :]), waits=w)
        xr = xrb[:, s4 * NOWN:(s4 + 1) * NOWN]
        xw = [sp_xr[dc]]
        lk = slot_last.pop(("xrb", s4), None)
        if lk is not None:
            xw.extend(lk if isinstance(lk, list) else [lk])
        rtk = sL.dma("s", call("dma_start", out=xr,
                               in_=xres[dc * 128:(dc + 1) * 128, :]), waits=xw)
        fload[dc] = (ftk, rtk)

    def fin_chunk_ops(dc):
        s4 = dc % 4
        s2 = dc % 2
        t16 = sq16[:, s2 * NOWN:(s2 + 1) * NOWN]
        xr = xrb[:, s4 * NOWN:(s4 + 1) * NOWN]
        ftk, rtk = fload.pop(dc)
        lks = slot_last.pop(("sq16", s2), None)
        lks = [] if lks is None else (lks if isinstance(lks, list) else [lks])
        steps = []
        for eng, c0, c1 in (("v", 0, 640), ("q", 640, 1024)):
            steps.append((eng, call("tensor_tensor", t16[:, c0:c1],
                                    fb16[:, s4 * NOWN + c0: s4 * NOWN + c1],
                                    x16[:, c0:c1], ALU.subtract),
                          [ftk, m316_tok] + lks, []))
            steps.append((eng, call("tensor_tensor", t16[:, c0:c1], t16[:, c0:c1],
                                    x16[:, NOWN + c0: NOWN + c1], ALU.mult),
                          [], []))
            steps.append((eng, call("tensor_tensor", xr[:, c0:c1], t16[:, c0:c1],
                                    xr[:, c0:c1], ALU.add), [rtk], ["tok"]))
        return steps

    def fin_fini(dc):
        s4 = dc % 4
        xr = xrb[:, s4 * NOWN:(s4 + 1) * NOWN]

        def f(toks):
            fin_done[dc] = toks
            tok = oF.dma("y", call("dma_start",
                                   out=outT[dc * 128:(dc + 1) * 128, :], in_=xr),
                         waits=toks)
            slot_last[("xrb", s4)] = tok
            slot_last[("fb16", s4)] = list(toks)
            slot_last[("sq16", dc % 2)] = list(toks)
        return f

    for dpre in range(3):
        fin_load(dpre)
    for dc in range(0, NKC, 2):
        emit_pairwise([fin_chunk_ops(dc), fin_chunk_ops(dc + 1)],
                      [fin_fini(dc), fin_fini(dc + 1)])
        for dn in (dc + 3, dc + 4):
            if dn < NKC:
                fin_load(dn)
    sch.op("y", call("nop"), waits=oF.totals())


def _mk_env(nc, sch, ctx):
    E = SimpleNamespace(nc=nc, sch=sch, ctx=ctx)
    E.dC = sch.sem("dC")
    E.yL = Lanes(sch, "yL", 4)
    E.qL = Lanes(sch, "qL", 2)
    E.oL = Lanes(sch, "oL", 2)
    E.pL = Lanes(sch, "pL", 4)
    E.sL = Lanes(sch, "sL", 4)
    E.vL = Lanes(sch, "vL", 2)
    E.oF = Lanes(sch, "oF", 2)
    E.sT = sch.sem("sT")
    E.sV = sch.sem("sV")
    E.sS = sch.sem("sS")
    E.sQ = sch.sem("sQ")
    sch.chain = {"v": E.sV, "s": E.sS, "q": E.sQ}
    return E


def _load_consts(E, srcs):
    nc, sch = E.nc, E.sch
    E.ident = E.ctx.enter_context(nc.sbuf_tensor("ident", [128, 128], BF16))
    E.ones = E.ctx.enter_context(nc.sbuf_tensor("ones", [128, 128], BF16))
    E.onesrow = E.ctx.enter_context(nc.sbuf_tensor("onesrow", [1, 128], BF16))
    E.negrow = E.ctx.enter_context(nc.sbuf_tensor("negrow", [1, 128], BF16))
    E.mask = E.ctx.enter_context(nc.sbuf_tensor("mask", [128, 128], BF16))
    E.maskT = E.ctx.enter_context(nc.sbuf_tensor("maskT", [128, 128], BF16))
    for dst, src in ((E.ident, srcs["identA"]), (E.ones, srcs["onesA"]),
                     (E.onesrow, srcs["onesrowA"]), (E.negrow, srcs["negrowA"]),
                     (E.mask, srcs["maskA"]), (E.maskT, srcs["maskTA"])):
        sch.op("y", call("dma_start", out=dst[:], in_=src[:]), incs=[(E.dC, 16)])
    E.consts = (E.dC, E.dC.n)


def _declare_const_drams(nc):
    return {
        "identA": nc.dram_tensor("identA", [128, 128], BF16, kind="ExternalInput").ap(),
        "onesA": nc.dram_tensor("onesA", [128, 128], BF16, kind="ExternalInput").ap(),
        "onesrowA": nc.dram_tensor("onesrowA", [1, 128], BF16, kind="ExternalInput").ap(),
        "negrowA": nc.dram_tensor("negrowA", [1, 128], BF16, kind="ExternalInput").ap(),
        "maskA": nc.dram_tensor("maskA", [128, 128], BF16, kind="ExternalInput").ap(),
        "maskTA": nc.dram_tensor("maskTA", [128, 128], BF16, kind="ExternalInput").ap(),
    }


def _declare_bc_drams(nc, internal_otf):
    d = {}
    if not internal_otf:
        d["OTf"] = nc.dram_tensor("OTf", [H * 128, NOWN], BF16, kind="ExternalInput").ap()
    d["xin"] = nc.dram_tensor("xin", [D, NOWN], F32, kind="ExternalInput").ap()
    d["wo_sw"] = nc.dram_tensor("wo_sw", [NKC * 128, D], F32, kind="ExternalInput").ap()
    d["WT_sw"] = nc.dram_tensor("WT_sw", [NKC * 128, D], F32, kind="ExternalInput").ap()
    d["A_sw"] = nc.dram_tensor("A_sw", [128, NKC * 256], F32, kind="ExternalInput").ap()
    d["B_sw"] = nc.dram_tensor("B_sw", [128, 2 * D], F32, kind="ExternalInput").ap()
    d["bias_sw"] = nc.dram_tensor("bias_sw", [128, NKC], F32, kind="ExternalInput").ap()
    d["probs_sw"] = nc.dram_tensor("probs_sw", [128, 2], F32, kind="ExternalInput").ap()
    d["outT"] = nc.dram_tensor("outT", [D, NOWN], F32, kind="ExternalOutput").ap()
    d["attnP"] = nc.dram_tensor("attnP", [D, NOWN], BF16, kind="Internal").ap()
    d["xres"] = nc.dram_tensor("xres", [D, NOWN], F32, kind="Internal").ap()
    d["ffnX"] = nc.dram_tensor("ffnX", [D, NOWN], BF16, kind="Internal").ap()
    return d


def build_fused():
    nc = bass.Bass("TRN2", target_bir_lowering=False)
    hsT = nc.dram_tensor("hsT", [D, S], F32, kind="ExternalInput").ap()
    wqkv = nc.dram_tensor("wqkv", [HPC * 128, NKC * 384], F32, kind="ExternalInput").ap()
    csrc = _declare_const_drams(nc)
    CCsh = nc.dram_tensor("CCsh", [H * 128, S], BF16, kind="Internal",
                          addr_space="Shared").ap()
    bbin = nc.dram_tensor("bbin", [1, 16], F32, kind="Internal").ap()
    bbout = nc.dram_tensor("bbout", [8, 16], F32, kind="Internal",
                           addr_space="Shared").ap()
    bc = _declare_bc_drams(nc, internal_otf=True)

    sch = Sched()
    ctx = ExitStack()
    with ctx:
        E = _mk_env(nc, sch, ctx)
        cc = sch.sem("cc")
        _load_consts(E, csrc)

        base_cell = {}

        def getpid(e):
            if "pid" not in base_cell:
                base_cell["pid"] = e.partition_id()
            return base_cell["pid"]

        def ot_store(h, src, waits):
            def fn(e, h=h, src=src):
                if "v" not in base_cell:
                    base_cell["v"] = (getpid(e) % 2) * (HPC * 128)
                dst = CCsh[bass.DynSlice(base_cell["v"], HPC * 128), :]
                return e.dma_start(out=dst[h * 128:(h + 1) * 128, :], in_=src)
            return E.oL.dma("q", fn, waits=waits)
        E.ot_store = ot_store
        E.hsT, E.wqkv = hsT, wqkv
        E.pre_wtok = {}

        def proj_hook(hh, p, tok):
            if hh == HPC - 1 and p == 5:
                E.pre_wtok[0] = E.qL.dma("q", lambda e: e.dma_start(
                    out=E.wbuf[:, 0:D], in_=bc["wo_sw"][0:128, :]), waits=[tok])
                E.pre_wtok[1] = E.qL.dma("q", lambda e: e.dma_start(
                    out=E.wbuf[:, D:2 * D], in_=bc["wo_sw"][128:256, :]), waits=[tok])
        E.proj_hook = proj_hook
        ascope = ExitStack()
        _phaseA(E, ascope)
        ascope.close()
        wbuf = ctx.enter_context(nc.sbuf_tensor("wbuf", [128, 2 * D], BF16))
        E.wbuf = wbuf
        btok = E.qL.dma("q", call("dma_start", out=bbin[:], in_=E.ones[0:1, 0:16]))
        sch.op("q", call("collective_compute", "AllGather", ALU.bypass,
                         [list(range(8))], [bbin.opt()], [bbout.opt()]),
               waits=E.oL.totals() + [btok], incs=[(cc, 1)], chainable=False)

        def otf_own(e):
            pid = getpid(e)
            return CCsh[bass.DynSlice((pid % 2) * (HPC * 128), HPC * 128),
                        bass.DynSlice((pid % 2) * NOWN, NOWN)].rearrange(
                "(hv p) t -> p hv t", p=128)

        def otf_mate(e):
            pid = getpid(e)
            return CCsh[bass.DynSlice(((pid + 1) % 2) * (HPC * 128), HPC * 128),
                        bass.DynSlice((pid % 2) * NOWN, NOWN)].rearrange(
                "(hv p) t -> p hv t", p=128)
        E.otf_own, E.otf_mate = otf_own, otf_mate
        E.otf_eng = "q"
        E.own_gate = E.oL.totals()
        E.gate = [(cc, 1)]
        for k in ("xin", "wo_sw", "WT_sw", "A_sw", "B_sw", "bias_sw", "probs_sw",
                  "outT", "attnP", "xres", "ffnX"):
            setattr(E, k, bc[k])
        _phaseBC(E)
        sch.emit(nc, ctx)
    return nc


def build_progA():
    nc = bass.Bass("TRN2", target_bir_lowering=False)
    hsT = nc.dram_tensor("hsT", [D, S], F32, kind="ExternalInput").ap()
    wqkv = nc.dram_tensor("wqkv", [HPC * 128, NKC * 384], F32, kind="ExternalInput").ap()
    csrc = _declare_const_drams(nc)
    OT = nc.dram_tensor("OT", [HPC * 128, S], BF16, kind="ExternalOutput").ap()
    sch = Sched()
    ctx = ExitStack()
    with ctx:
        E = _mk_env(nc, sch, ctx)
        _load_consts(E, csrc)

        def ot_store(h, src, waits):
            return E.oL.dma("y", call("dma_start", out=OT[h * 128:(h + 1) * 128, :],
                                      in_=src), waits=waits)
        E.ot_store = ot_store
        E.hsT, E.wqkv = hsT, wqkv
        ascope = ExitStack()
        _phaseA(E, ascope)
        sch.op("y", call("nop"), waits=E.oL.totals())
        ascope.close()
        sch.emit(nc, ctx)
    return nc


def build_progB():
    nc = bass.Bass("TRN2", target_bir_lowering=False)
    csrc = _declare_const_drams(nc)
    bc = _declare_bc_drams(nc, internal_otf=False)
    sch = Sched()
    ctx = ExitStack()
    with ctx:
        E = _mk_env(nc, sch, ctx)
        _load_consts(E, csrc)
        OTf = bc["OTf"]
        E.otf_own = lambda e: OTf[0:HPC * 128, :].rearrange("(hv p) t -> p hv t", p=128)
        E.otf_mate = lambda e: OTf[HPC * 128:, :].rearrange("(hv p) t -> p hv t", p=128)
        E.own_gate = []
        E.otf_eng = "y"
        E.gate = []
        for k in ("xin", "wo_sw", "WT_sw", "A_sw", "B_sw", "bias_sw", "probs_sw",
                  "outT", "attnP", "xres", "ffnX"):
            setattr(E, k, bc[k])
        _phaseBC(E)
        sch.emit(nc, ctx)
    return nc


def _sw(arr2d):
    K, M = arr2d.shape
    return np.ascontiguousarray(
        arr2d.reshape(K // 128, 128, M).transpose(1, 0, 2).reshape(128, -1))


_prog = None


def kernel(L0_attn_hidden_states, L0_attn_wq, L0_attn_wkv, L0_attn_wo,
           L0_ffn_W, L0_ffn_b, L0_ffn_A, L0_ffn_B, L0_ffn_probs):
    global _prog
    hs = np.ascontiguousarray(L0_attn_hidden_states, dtype=np.float32)
    wq = np.asarray(L0_attn_wq, dtype=np.float32)
    wkv = np.asarray(L0_attn_wkv, dtype=np.float32)
    wo = np.asarray(L0_attn_wo, dtype=np.float32)
    W = np.asarray(L0_ffn_W, dtype=np.float32)
    bia = np.asarray(L0_ffn_b, dtype=np.float32)
    A = np.asarray(L0_ffn_A, dtype=np.float32)
    Bw = np.asarray(L0_ffn_B, dtype=np.float32)
    probs = np.asarray(L0_ffn_probs, dtype=np.float32)

    ident = np.eye(128, dtype=BF)
    ones128 = np.ones((128, 128), dtype=BF)
    onesrow = np.ones((1, 128), dtype=BF)
    negrow = np.full((1, 128), -1.0, dtype=BF)
    ii, jj = np.indices((128, 128))
    mask = np.where(jj <= ii, 0.0, NEG).astype(BF)
    maskT = np.ascontiguousarray(mask.T)

    hsT = [np.ascontiguousarray(hs[b].T) for b in range(B)]
    wqkv_p = []
    for p in range(2):
        blocks = []
        for hl in range(HPC):
            g = p * HPC + hl
            blk = np.concatenate([
                _sw(wq[:, g * DQ: g * DQ + DN]),
                _sw(wkv[:, g * DN: (g + 1) * DN]),
                _sw(wkv[:, H * DN + g * DV: H * DN + (g + 1) * DV])], axis=1)
            blocks.append(blk)
        wqkv_p.append(np.ascontiguousarray(np.concatenate(blocks, axis=0)))

    wo_sw_p = []
    for p in range(2):
        order = list(range(p * HPC, (p + 1) * HPC)) + \
            list(range((1 - p) * HPC, (2 - p) * HPC))
        wo_perm = np.concatenate([wo[g * DV:(g + 1) * DV, :] for g in order], axis=0)
        wo_sw_p.append(np.ascontiguousarray(np.concatenate(
            [_sw(wo_perm[:, dc * 128:(dc + 1) * 128]) for dc in range(NKC)], axis=0)))
    WT_sw = np.ascontiguousarray(np.concatenate(
        [_sw(np.ascontiguousarray(W[oc * 128:(oc + 1) * 128, :].T)) for oc in range(NKC)], axis=0))
    A_sw = _sw(np.ascontiguousarray(A.transpose(2, 0, 1).reshape(D, 256)))
    B_sw = np.ascontiguousarray(
        Bw.transpose(0, 2, 1).reshape(256, D).reshape(2, 128, D)
        .transpose(1, 0, 2).reshape(128, 2 * D))
    bias_sw = np.ascontiguousarray(bia.reshape(NKC, 128).T)
    probs_sw = np.ascontiguousarray(np.repeat(probs, 16).reshape(2, 128).T)

    if _prog is None:
        _prog = build_fused()
    in_maps = []
    for c in range(8):
        b, p = c // 2, c % 2
        own = slice(p * NOWN, (p + 1) * NOWN)
        in_maps.append(dict(
            hsT=hsT[b], wqkv=wqkv_p[p],
            identA=ident, onesA=ones128, onesrowA=onesrow, negrowA=negrow,
            maskA=mask, maskTA=maskT,
            xin=np.ascontiguousarray(hsT[b][:, own]), wo_sw=wo_sw_p[p], WT_sw=WT_sw,
            A_sw=A_sw, B_sw=B_sw, bias_sw=bias_sw, probs_sw=probs_sw))
    res = run_bass_kernel_spmd(_prog, in_maps, core_ids=list(range(8)))

    out = np.empty((B, S, D), dtype=np.float32)
    for c in range(8):
        b, p = c // 2, c % 2
        out[b, p * NOWN:(p + 1) * NOWN, :] = res.results[c]["outT"].T
    return out


# version 24
# speedup vs baseline: 1.1078x; 1.0964x over previous
import numpy as np
import ml_dtypes
from contextlib import ExitStack
from types import SimpleNamespace

import concourse.bass as bass
import concourse.mybir as mybir
from concourse.bass_utils import run_bass_kernel_spmd

F32 = mybir.dt.float32
BF16 = mybir.dt.bfloat16
I32 = mybir.dt.int32
AF = mybir.ActivationFunctionType
ALU = mybir.AluOpType
AX = mybir.AxisListType
BF = ml_dtypes.bfloat16

DBG = dict(const_c=False, no_lsum=False, norm_tail_const=False,
           norm_no_reduce=False, norm_no_square=False,
           norm_no_rounds=False)

H, DN, DV, DQ = 32, 128, 128, 192
D = 4096
S = 2048
B = 4
HPC = 16
NOWN = 1024
NKC = D // 128
SCALE = float(np.float32(DQ) ** -0.5)
NEG = -1.0e30
EPS = 1e-5
GROUPS = [[0, 1], [2, 3], [4, 5], [6, 7]]


class Sem:
    __slots__ = ("name", "n", "h")

    def __init__(self, name):
        self.name, self.n, self.h = name, 0, None


class Sched:
    ENGS = ("t", "v", "s", "q", "y")

    def __init__(self):
        self.ops = {k: [] for k in self.ENGS}
        self.sems = []
        self.chain = {}

    def sem(self, name):
        s = Sem(name)
        self.sems.append(s)
        return s

    def op(self, eng, fn, waits=(), incs=(), chainable=True):
        ws = []
        for w in waits:
            if isinstance(w, tuple):
                ws.append((w[0], w[1]))
            else:
                ws.append((w, w.n))
        cs = self.chain.get(eng) if chainable else None
        if cs is not None and cs.n > 0:
            ws.append((cs, cs.n))
        ics = []
        for s, k in incs:
            s.n += k
            ics.append((s, k))
        if cs is not None and not any(s is cs for s, _ in ics):
            cs.n += 1
            ics.append((cs, 1))
        self.ops[eng].append((fn, ws, ics))

    def emit(self, nc, ctx):
        for s in self.sems:
            s.h = ctx.enter_context(nc.semaphore(s.name))
        block = ctx.enter_context(nc.Block())
        deco = dict(t=block.tensor, v=block.vector, s=block.scalar,
                    q=block.gpsimd, y=block.sync)

        def runner(ops):
            def run(eng):
                waited = {}
                for fn, ws, ics in ops:
                    for s, v in ws:
                        if v > 0 and waited.get(s, 0) < v:
                            eng.wait_ge(s.h, v)
                            waited[s] = v
                    inst = fn(eng)
                    for s, k in ics:
                        inst = inst.then_inc(s.h, k)
            return run
        for key in self.ENGS:
            deco[key](runner(list(self.ops[key])))


class Lanes:

    def __init__(self, sch, name, n):
        self.sch = sch
        self.sems = [sch.sem(f"{name}{i}") for i in range(n)]
        self.n = n
        self.i = 0

    def dma(self, eng, fn, waits=()):
        s = self.sems[self.i % self.n]
        self.i += 1
        w = list(waits)
        if s.n > 0:
            w.append((s, s.n))
        self.sch.op(eng, fn, waits=w, incs=[(s, 16)], chainable=False)
        return (s, s.n)

    def totals(self):
        return [(s, s.n) for s in self.sems if s.n > 0]


def call(name, *args, **kw):
    return lambda e, n=name, a=args, k=kw: getattr(e, n)(*a, **k)


def _phaseA(E, actx):
    sch, nc = E.sch, E.nc
    sT, sV, sS, sQ = E.sT, E.sV, E.sS, E.sQ
    yL, qL, oL = E.yL, E.qL, E.oL
    ident, ones, onesrow, mask, maskT = E.ident, E.ones, E.onesrow, E.mask, E.maskT
    negrow = E.negrow
    hsT, wqkv = E.hsT, E.wqkv
    consts = E.consts

    xb = actx.enter_context(nc.sbuf_tensor("xb", [128, NKC * S], BF16))
    wq_sb = actx.enter_context(nc.sbuf_tensor("wq_sb", [128, NKC * 128], BF16))
    wk_sb = actx.enter_context(nc.sbuf_tensor("wk_sb", [128, NKC * 128], BF16))
    wv_sb = actx.enter_context(nc.sbuf_tensor("wv_sb", [128, NKC * 128], BF16))
    wcol = (wq_sb, wk_sb, wv_sb)
    cdma = {}

    def dma_col(h, col, waits=()):
        cdma[(h, col)] = qL.dma("q", call(
            "dma_start", out=wcol[col][:],
            in_=wqkv[h * 128:(h + 1) * 128, col * (NKC * 128):(col + 1) * (NKC * 128)]),
            waits=waits)

    for c in range(3):
        dma_col(0, c)

    with (nc.sbuf_tensor("xstage", [128, 2 * S], F32) as xstage,
          nc.sbuf_tensor("xsq", [128, 2 * S], BF16) as xsq,
          nc.sbuf_tensor("m_sb", [128, S], F32) as m_sb,
          nc.sbuf_tensor("e2_sb", [128, S], F32) as e2_sb,
          nc.sbuf_tensor("r_sb", [128, S], F32) as r_sb,
          nc.psum_tensor("psSum", [128, S], F32) as psSum,
          nc.psum_tensor("psSq", [128, S], F32) as psSq):
        vdone, sdone, tdone = {}, {}, {}
        slots = [xstage[:, 0:S], xstage[:, S:2 * S], r_sb[:]]
        for kc in range(NKC):
            pg = kc % 2
            stg = slots[kc % 3]
            waits = []
            if kc >= 3:
                waits = [(sV, vdone[kc - 3]), (sS, sdone[kc - 3])]
            xeng = "y" if kc % 2 == 0 else "s"
            xlane = yL if kc % 2 == 0 else E.sL
            tok = xlane.dma(xeng, call("dma_start", out=stg,
                                       in_=hsT[kc * 128:(kc + 1) * 128, :]), waits=waits)
            sch.op("v", call("tensor_copy", xb[:, kc * S:(kc + 1) * S], stg),
                   waits=[tok], incs=[(sV, 1)])
            vdone[kc] = sV.n
            swaits = [tok] if kc < 2 else [tok, (sT, tdone[kc - 2])]
            sch.op("s", call("activation", xsq[:, pg * S:(pg + 1) * S],
                             stg, AF.Square),
                   waits=swaits, incs=[(sS, 1)])
            sdone[kc] = sS.n
            for tc in range(4):
                sch.op("t", call("matmul", psSum[:, tc * 512:(tc + 1) * 512],
                                 ones[:], xb[:, kc * S + tc * 512: kc * S + (tc + 1) * 512],
                                 start=(kc == 0), stop=(kc == NKC - 1)),
                       waits=[sV, sS, consts] if tc == 0 else [])
            for tc in range(4):
                sch.op("t", call("matmul", psSq[:, tc * 512:(tc + 1) * 512],
                                 ones[:], xsq[:, pg * S + tc * 512: pg * S + (tc + 1) * 512],
                                 start=(kc == 0), stop=(kc == NKC - 1)),
                       incs=[(sT, 1)] if tc == 3 else [])
            tdone[kc] = sT.n
        sch.op("v", call("tensor_scalar", m_sb[:], psSum[:], 1.0 / D, None, ALU.mult),
               waits=[(sT, sT.n)], incs=[(sV, 1)])
        sch.op("v", call("tensor_scalar", e2_sb[:], psSq[:], 1.0 / D, None, ALU.mult),
               incs=[(sV, 1)])
        sch.op("s", call("activation", r_sb[:], m_sb[:], AF.Square),
               waits=[sV], incs=[(sS, 1)])
        sch.op("v", call("tensor_tensor", e2_sb[:], e2_sb[:], r_sb[:], ALU.subtract),
               waits=[sS])
        sch.op("v", call("tensor_scalar", e2_sb[:], e2_sb[:], EPS, None, ALU.add),
               incs=[(sV, 1)])
        sch.op("s", call("activation", r_sb[:], e2_sb[:], AF.Ln),
               waits=[sV], incs=[(sS, 1)])
        sch.op("s", call("activation", r_sb[:], r_sb[:], AF.Exp, scale=-0.5),
               incs=[(sS, 1)])
        m16 = xsq[:, 0:S]
        r16 = xsq[:, S:2 * S]
        sch.op("v", call("tensor_copy", m16, m_sb[:]), waits=[(sT, tdone[NKC - 1])])
        sch.op("v", call("tensor_copy", r16, r_sb[:]), waits=[sS], incs=[(sV, 1)])
        mr_tok = sV.n
        vnormk = {}
        for kc in range(NKC):
            eng = "v" if kc % 2 == 0 else "q"
            w0 = [] if eng == "v" else [(sV, mr_tok), (sT, tdone[kc])]
            sch.op(eng, call("tensor_tensor", xb[:, kc * S:(kc + 1) * S],
                             xb[:, kc * S:(kc + 1) * S], m16, ALU.subtract),
                   waits=w0)
            sch.op(eng, call("tensor_tensor", xb[:, kc * S:(kc + 1) * S],
                             xb[:, kc * S:(kc + 1) * S], r16, ALU.mult),
                   incs=[(sV, 1)] if eng == "v" else [(E.sQ, 1)])
            vnormk[kc] = (sV if eng == "v" else E.sQ, (sV if eng == "v" else E.sQ).n)
        E.lnvnorm = sV.n
        E.lnqnorm = E.sQ.n
        E.vnormk = vnormk

    qT2 = actx.enter_context(nc.sbuf_tensor("qT2", [128, 2 * S], BF16))
    kT2 = actx.enter_context(nc.sbuf_tensor("kT2", [128, 2 * S], BF16))
    vT = actx.enter_context(nc.sbuf_tensor("vT", [128, S], BF16))
    Vtok = actx.enter_context(nc.sbuf_tensor("Vtok", [128, S], BF16))
    PT = actx.enter_context(nc.sbuf_tensor("PT", [128, 16 * 512], BF16))
    OTb = actx.enter_context(nc.sbuf_tensor("OTb", [128, 2 * S], BF16))
    qsq = actx.enter_context(nc.sbuf_tensor("qsq", [128, S // 2], BF16))
    nr32 = actx.enter_context(nc.sbuf_tensor("nr32", [128, 16], F32))
    cscr = actx.enter_context(nc.sbuf_tensor("cscr", [1, 8], F32))
    nCb = actx.enter_context(nc.sbuf_tensor("nCb", [128, 2], F32))
    cs16 = actx.enter_context(nc.sbuf_tensor("cs16", [1, 2], BF16))
    linvb = actx.enter_context(nc.sbuf_tensor("linvb", [128, 512], BF16))
    lrow32 = actx.enter_context(nc.sbuf_tensor("lrow32", [1, 512], F32))
    lrow16 = actx.enter_context(nc.sbuf_tensor("lrow16", [1, 512], BF16))

    psP = actx.enter_context(nc.psum_tensor("psP", [128, 1024], F32))
    psS2a = actx.enter_context(nc.psum_tensor("psS2a", [128, 512], F32))
    psS2b = actx.enter_context(nc.psum_tensor("psS2b", [128, 512], F32))
    psOa = actx.enter_context(nc.psum_tensor("psOa", [128, 512], F32))
    psOb = actx.enter_context(nc.psum_tensor("psOb", [128, 512], F32))
    psN = actx.enter_context(nc.psum_tensor("psN", [128, 512], F32))
    psLB = actx.enter_context(nc.psum_tensor("psLB", [128, 512], F32))

    st = SimpleNamespace(evac=0, avdone=0, ot_tok={}, trtok={},
                         projtok={}, tr_last=[], wq=None,
                         avls={}, expbank={0: 0, 1: 0}, otmult={0: 0, 1: 0},
                         linv_evac=0, ncb_tok={0: 0, 1: 0},
                         ncb_war={0: 0, 1: 0}, psn_war=0, psn_bc_war=0,
                         cscr_war=0, qsq_war=0, cscr_lnexp=0, normgens=[])

    def emit_proj_mms(h, part):
        col, tp = divmod(part, 2)
        buf = wcol[col]
        for kc in range(NKC):
            for th in range(2):
                tc = tp * 2 + th
                w0 = []
                if kc == 0 and th == 0:
                    w0 = [cdma[(h, col)], (sS, st.evac)]
                    if not (h == 0 and part == 0):
                        w0 += [(sV, E.lnvnorm), (E.sQ, E.lnqnorm)]
                if h == 0 and part == 0 and th == 0:
                    w0.append(E.vnormk[kc])
                sch.op("t", call("matmul", psP[:, th * 512:(th + 1) * 512],
                                 buf[:, kc * 128:(kc + 1) * 128],
                                 xb[:, kc * S + tc * 512: kc * S + (tc + 1) * 512],
                                 start=(kc == 0), stop=(kc == NKC - 1)),
                       waits=w0,
                       incs=[(sT, 1)] if kc == NKC - 1 and th == 1 else [])
            if kc % 8 == 7 and kc != NKC - 1:
                yield
        st.projtok[(h, part)] = sT.n

    def emit_proj_part(h, part):
        for _ in emit_proj_mms(h, part):
            pass
        finish_proj_part(h, part)

    def finish_proj_part(h, part):
        col, tp = divmod(part, 2)
        hp = h % 2
        if col == 0:
            dst = qT2[:, hp * S + tp * 1024: hp * S + (tp + 1) * 1024]
            sch.op("s", call("mul", dst, psP[:], SCALE), waits=[sT], incs=[(sS, 1)])
        elif col == 1:
            dst = kT2[:, hp * S + tp * 1024: hp * S + (tp + 1) * 1024]
            sch.op("s", call("copy", dst, psP[:]), waits=[sT], incs=[(sS, 1)])
        else:
            dst = vT[:, tp * 1024:(tp + 1) * 1024]
            sch.op("s", call("copy", dst, psP[:]),
                   waits=[sT] + list(st.tr_last), incs=[(sS, 1)])
        st.evac = sS.n
    def emit_vtrans(h):
        toks = {}
        for kt in range(16):
            w = [(sS, st.evac)] if kt == 0 else []
            if kt == 0 and st.avdone:
                w.append((sT, st.avdone))
            toks[kt] = yL.dma("y", call(
                "dma_start_transpose", Vtok[:, kt * 128:(kt + 1) * 128],
                vT[:, kt * 128:(kt + 1) * 128]), waits=w)
        st.trtok[h] = toks
        st.tr_last = [toks[kt] for kt in (12, 13, 14, 15)]

    onescol = E.ones[:, 0:1]

    def norm_item(h, qk_tok):
        hp = (h % 2) * S
        hc = h % 2
        sq_pe = 0
        first = True
        for half, src in ((0, qT2), (1, kT2)):
            for piece in range(2):
                sqtok = 0
                if not DBG["norm_no_square"]:
                    if first:
                        sqw = [qk_tok]
                        if st.qsq_war:
                            sqw.append((sT, st.qsq_war))
                        first = False
                    else:
                        sqw = [(sT, sq_pe)]
                    sch.op("s", call("activation", qsq[:],
                                     src[:, hp + piece * 1024: hp + (piece + 1) * 1024],
                                     AF.Square), waits=sqw, incs=[(sS, 1)])
                    sqtok = sS.n
                    yield
                if DBG["norm_no_rounds"]:
                    continue
                for c in range(2):
                    w = []
                    if c == 0:
                        if sqtok:
                            w.append((sS, sqtok))
                        if st.psn_war:
                            w.append((sV, st.psn_war))
                        if st.psn_bc_war:
                            w.append((sV, st.psn_bc_war))
                    sch.op("t", call("matmul", psN[:, 0:512], ones[:],
                                     qsq[:, c * 512:(c + 1) * 512],
                                     start=(c == 0), stop=(c == 1)),
                           waits=w, incs=[(sT, 1)] if c == 1 else [])
                sq_pe = sT.n
                nri = half * 2 + piece
                if not DBG["norm_no_reduce"]:
                    sch.op("v", call("tensor_reduce", nr32[:, nri:nri + 1],
                                     psN[:, 0:512], AX.X, ALU.max,
                                     negate=True),
                           waits=[(sT, sq_pe)], incs=[(sV, 1)])
                    st.psn_war = sV.n
                yield
        st.qsq_war = sq_pe
        if DBG["norm_tail_const"]:
            ncw2 = [(sS, st.ncb_war[hc])] if st.ncb_war[hc] else []
            sch.op("v", call("memset", nCb[:, hc:hc + 1], -30.0),
                   waits=ncw2, incs=[(sV, 1)])
            st.ncb_tok[hc] = sV.n
            return
        sch.op("v", call("tensor_reduce", cscr[0:1, 0:1], nr32[0:1, 0:2],
                         AX.X, ALU.min),
               waits=[(sS, st.cscr_lnexp)] if st.cscr_lnexp else [])
        sch.op("v", call("tensor_reduce", cscr[0:1, 1:2], nr32[0:1, 2:4],
                         AX.X, ALU.min))
        sch.op("v", call("tensor_tensor", cscr[0:1, 2:3], cscr[0:1, 0:1],
                         cscr[0:1, 1:2], ALU.mult), incs=[(sV, 1)])
        prod_tok = sV.n
        cw = [(sV, prod_tok)]
        if st.cscr_war:
            cw.append((sT, st.cscr_war))
        sch.op("s", call("activation", cscr[0:1, 3:4], cscr[0:1, 2:3], AF.Ln),
               waits=cw, incs=[(sS, 1)])
        sch.op("s", call("activation", cscr[0:1, 4:5], cscr[0:1, 3:4], AF.Exp,
                         scale=0.5), incs=[(sS, 1)])
        c_tok = sS.n
        st.cscr_lnexp = c_tok
        yield
        cw2 = [(sS, c_tok)]
        if st.cscr_war:
            cw2.append((sT, st.cscr_war))
        sch.op("v", call("tensor_copy", cs16[0:1, 0:1], cscr[0:1, 4:5]),
               waits=cw2, incs=[(sV, 1)])
        bw = [(sV, sV.n)]
        if st.psn_bc_war:
            bw.append((sV, st.psn_bc_war))
        sch.op("t", call("matmul", psN[:, 1:2], negrow[:], cs16[0:1, 0:1],
                         start=True, stop=True), waits=bw, incs=[(sT, 1)])
        st.cscr_war = sT.n
        ncw = [(sT, st.cscr_war)]
        if st.ncb_war[hc]:
            ncw.append((sS, st.ncb_war[hc]))
        sch.op("v", call("tensor_copy", nCb[:, hc:hc + 1], psN[:, 1:2]),
               waits=ncw, incs=[(sV, 1)])
        st.ncb_tok[hc] = sV.n
        st.psn_bc_war = sV.n

    def emit_head_attn(h, qkv_ready):
        hp = (h % 2) * S
        hc = h % 2
        exp_tok = {}
        lsum_tok = {}

        def finalize(g):
            psO = psOa if g % 2 == 0 else psOb
            if DBG["no_lsum"]:
                ob = hc * S
                otw = [(sT, lsum_tok[g])]
                if g == 0 and h >= 2 and (h - 2) in st.ot_tok:
                    otw.append(st.ot_tok[h - 2])
                sch.op("v", call("tensor_copy",
                                 OTb[:, ob + g * 512: ob + (g + 1) * 512],
                                 psO[:, 0:512]), waits=otw, incs=[(sV, 1)])
                st.otmult[g % 2] = sV.n
                return
            sch.op("v", call("reciprocal", lrow32[:], psLB[0:1, 0:512]),
                   waits=[(sT, lsum_tok[g])])
            sch.op("v", call("tensor_copy", lrow16[:], lrow32[:]), incs=[(sV, 1)])
            sch.op("t", call("matmul", psLB[:, 0:512], onesrow[:], lrow16[0:1, :],
                             start=True, stop=True), waits=[(sV, sV.n)],
                   incs=[(sT, 1)])
            sch.op("v", call("tensor_copy", linvb[:], psLB[:]),
                   waits=[(sT, sT.n)], incs=[(sV, 1)])
            st.linv_evac = sV.n
            ob = hc * S
            otw = []
            if g == 0 and h >= 2 and (h - 2) in st.ot_tok:
                otw = [st.ot_tok[h - 2]]
            sch.op("v", call("tensor_tensor", OTb[:, ob + g * 512: ob + (g + 1) * 512],
                             psO[:, 0:512], linvb[:], ALU.mult),
                   waits=otw, incs=[(sV, 1)])
            st.otmult[g % 2] = sV.n

        def emit_avls(g, j):
            off = max(0, (j - 4 * g)) * 128
            psO = psOa if g % 2 == 0 else psOb
            w = [(sS, exp_tok[j])]
            if j == 0 and st.otmult[g % 2]:
                w.append((sV, st.otmult[g % 2]))
            tk = st.trtok[h].pop(j, None)
            if tk is not None:
                w.append(tk)
            sch.op("t", call("matmul", psO[:, off:512],
                             Vtok[:, j * 128:(j + 1) * 128],
                             PT[:, j * 512 + off: (j + 1) * 512],
                             start=(j == 0), stop=(j == 4 * g + 3)),
                   waits=w, incs=[(sT, 1)])
            if not DBG["no_lsum"]:
                lw = []
                if j == 0 and st.linv_evac:
                    lw.append((sV, st.linv_evac))
                sch.op("t", call("matmul", psLB[0:1, off:512], onescol,
                                 PT[:, j * 512 + off: (j + 1) * 512],
                                 start=(j == 0), stop=(j == 4 * g + 3)),
                       waits=lw, incs=[(sT, 1)])
            st.avls[j] = sT.n

        for g in range(4):
            nk = 4 * g + 4
            for kt in range(nk):
                ps2 = psS2a if kt % 2 == 0 else psS2b
                off = max(0, (kt - 4 * g)) * 128
                w = []
                if g == 0 and kt == 0:
                    w = [(sS, qkv_ready), (sV, st.ncb_tok[hc])]
                if st.expbank[kt % 2]:
                    w.append((sS, st.expbank[kt % 2]))
                sch.op("t", call("matmul", ps2[:, off:512],
                                 kT2[:, hp + kt * 128: hp + (kt + 1) * 128],
                                 qT2[:, hp + g * 512 + off: hp + (g + 1) * 512],
                                 start=True, stop=True), waits=w, incs=[(sT, 1)])
                mm_tok = sT.n
                ew = [(sT, mm_tok)]
                if st.avls.get(kt):
                    ew.append((sT, st.avls[kt]))
                if g == 0 and kt == 0:
                    ew.append((sV, st.ncb_tok[hc]))
                if kt >= 4 * g:
                    sch.op("v", call("tensor_tensor", ps2[:, off:off + 128],
                                     ps2[:, off:off + 128], maskT[:], ALU.add),
                           waits=[(sT, mm_tok)], incs=[(sV, 1)])
                    ew.append((sV, sV.n))
                sch.op("s", call("activation", PT[:, kt * 512 + off: (kt + 1) * 512],
                                 ps2[:, off:512], AF.Exp, bias=nCb[:, hc:hc + 1]),
                       waits=ew, incs=[(sS, 1)])
                exp_tok[kt] = sS.n
                st.expbank[kt % 2] = sS.n
                if g > 0 and kt == 0:
                    finalize(g - 1)
                if kt >= 2:
                    emit_avls(g, kt - 2)
                st.wq.pump()
            emit_avls(g, nk - 2)
            emit_avls(g, nk - 1)
            lsum_tok[g] = sT.n
            st.avdone = sT.n
            st.wq.pump()
        st.wq.pump()
        finalize(3)
        st.ncb_war[hc] = sS.n
        st.ot_tok[h] = E.ot_store(h, OTb[:, hc * S:(hc + 1) * S], [(sV, sV.n)])

    def proj_item(h):
        for part in range(6):
            for _ in emit_proj_mms(h, part):
                yield
            finish_proj_part(h, part)
            if part == 3 and not DBG["const_c"]:
                wq.push("norm", norm_item(h, (sS, st.evac)))
            if part in (1, 3, 5) and h + 1 < HPC:
                dma_col(h + 1, part // 2, waits=[(sT, st.projtok[(h, part)])])
            hook = getattr(E, "proj_hook", None)
            if hook is not None:
                hook(h, part, (sT, st.projtok[(h, part)]))
            yield

    def vtrans_item(h):
        emit_vtrans(h)
        yield

    class WorkQueue:

        def __init__(self):
            self.items = []

        def push(self, kind, gen):
            self.items.append((kind, gen))

        def push_front(self, kind, gen):
            self.items.insert(0, (kind, gen))

        def pump(self):
            while self.items:
                kind, gen = self.items[0]
                try:
                    next(gen)
                    return
                except StopIteration:
                    self.items.pop(0)

        def drain_kind(self, kinds):
            while self.items and self.items[0][0] in kinds:
                kind, gen = self.items[0]
                try:
                    next(gen)
                except StopIteration:
                    self.items.pop(0)

        def drain_where(self, kind):
            for i in range(len(self.items) - 1, -1, -1):
                if self.items[i][0] == kind:
                    gen = self.items[i][1]
                    while True:
                        try:
                            next(gen)
                        except StopIteration:
                            break
                    self.items.pop(i)

        def drain(self):
            while self.items:
                self.pump()

    wq = WorkQueue()
    st.wq = wq

    for _ in proj_item(0):
        pass
    emit_vtrans(0)
    if DBG["const_c"]:
        sch.op("v", call("memset", nCb[:, 0:2], -30.0), incs=[(sV, 1)])
        st.ncb_tok[0] = st.ncb_tok[1] = sV.n

    for h in range(HPC):
        qkv_ready = st.evac
        wq.drain_where("norm")
        wq.drain_where("vtrans")
        if h + 1 < HPC:
            wq.push("proj", proj_item(h + 1))
        emit_head_attn(h, qkv_ready)
        wq.drain_where("proj")
        if h + 1 < HPC:
            wq.items.insert(min(1, len(wq.items)), ("vtrans", vtrans_item(h + 1)))
        else:
            wq.drain()
    E.projlast = (E.sT, st.projtok[(HPC - 1, 5)])


def _phaseBC(E):
    sch, nc, ctx = E.sch, E.nc, E.ctx
    sT, sV, sS, sQ = E.sT, E.sV, E.sS, E.sQ
    yL, pL, sL, qL, oF = E.yL, E.pL, E.sL, E.qL, E.oF
    ones = E.ones
    xin, wo_sw, WT_sw, A_sw, B_sw, bias_sw, probs_sw = (
        E.xin, E.wo_sw, E.WT_sw, E.A_sw, E.B_sw, E.bias_sw, E.probs_sw)
    outT, attnP, xres, ffnX = E.outT, E.attnP, E.xres, E.ffnX
    gate = list(E.gate)

    def lnfin(m_sl, r_sl, ps_sum, ps_sq):
        sch.op("v", call("tensor_scalar", m_sl, ps_sum[:], 1.0 / D, None, ALU.mult),
               waits=[(sT, sT.n)], incs=[(sV, 1)])
        sch.op("v", call("tensor_scalar", tmp[:], ps_sq[:], 1.0 / D, None, ALU.mult),
               incs=[(sV, 1)])
        sch.op("s", call("activation", r_sl, m_sl, AF.Square), waits=[sV], incs=[(sS, 1)])
        sch.op("v", call("tensor_tensor", tmp[:], tmp[:], r_sl, ALU.subtract),
               waits=[sS])
        sch.op("v", call("tensor_scalar", tmp[:], tmp[:], EPS, None, ALU.add),
               incs=[(sV, 1)])
        sch.op("s", call("activation", tmp[:], tmp[:], AF.Ln),
               waits=[sV], incs=[(sS, 1)])
        sch.op("s", call("activation", r_sl, tmp[:], AF.Exp, scale=-0.5),
               incs=[(sS, 1)])
        return (sS, sS.n)

    wbuf = getattr(E, "wbuf", None)
    if wbuf is None:
        wbuf = ctx.enter_context(nc.sbuf_tensor("wbuf", [128, 2 * D], BF16))
    aP = ctx.enter_context(nc.sbuf_tensor("aP", [128, 4 * NOWN], F32))
    x16 = ctx.enter_context(nc.sbuf_tensor("x16", [128, 2 * NOWN], BF16))
    sq16 = ctx.enter_context(nc.sbuf_tensor("sq16", [128, 2 * NOWN], BF16))
    xinb = ctx.enter_context(nc.sbuf_tensor("xinb", [128, 4 * NOWN], F32))
    xrb = ctx.enter_context(nc.sbuf_tensor("xrb", [128, 4 * NOWN], F32))
    m_a = ctx.enter_context(nc.sbuf_tensor("m_a", [128, NOWN], F32))
    r_a = ctx.enter_context(nc.sbuf_tensor("r_a", [128, NOWN], F32))
    m_2 = ctx.enter_context(nc.sbuf_tensor("m_2", [128, NOWN], F32))
    r_2 = ctx.enter_context(nc.sbuf_tensor("r_2", [128, NOWN], F32))
    m_3 = ctx.enter_context(nc.sbuf_tensor("m_3", [128, NOWN], F32))
    r_3 = ctx.enter_context(nc.sbuf_tensor("r_3", [128, NOWN], F32))
    tmp = aP[:, 0:NOWN]
    fb16 = ctx.enter_context(nc.sbuf_tensor("fb16", [128, 4 * NOWN], BF16))
    mr16 = fb16
    bias_sb = ctx.enter_context(nc.sbuf_tensor("bias_sb", [128, NKC], F32))
    probs_sb = ctx.enter_context(nc.sbuf_tensor("probs_sb", [128, 2], F32))

    psW = ctx.enter_context(nc.psum_tensor("psW", [128, 2048], F32))
    psSum = ctx.enter_context(nc.psum_tensor("psSumB", [128, 1024], F32))
    psSq = ctx.enter_context(nc.psum_tensor("psSqB", [128, 1024], F32))

    bscope = ExitStack()
    OTsb = bscope.enter_context(nc.sbuf_tensor("OTsb", [128, H * NOWN], BF16))

    otf_lane = yL if E.otf_eng == "y" else qL
    ot_own = otf_lane.dma(E.otf_eng, lambda e: e.dma_start(
        out=OTsb[:, 0:HPC * NOWN].rearrange("p (hv t) -> p hv t", hv=HPC),
        in_=E.otf_own(e)), waits=E.own_gate)
    ot_mate = otf_lane.dma(E.otf_eng, lambda e: e.dma_start(
        out=OTsb[:, HPC * NOWN:].rearrange("p (hv t) -> p hv t", hv=HPC),
        in_=E.otf_mate(e)), waits=gate)
    sch.op("y", call("dma_start", out=bias_sb[:], in_=bias_sw[:]),
           incs=[(E.dC, 16)])
    sch.op("y", call("dma_start", out=probs_sb[:], in_=probs_sw[:]),
           incs=[(E.dC, 16)])
    cn = (E.dC, E.dC.n)

    wtok = dict(getattr(E, "pre_wtok", {}))
    slot_last = {}
    sp_at, sp_xr, sp_ff = {}, {}, {}
    statT = {}
    fin_done = {}
    stat_q = []

    def emit_stats(ps_sum, ps_sq, ph, dc, pg, vtok, stok):
        if ph == "l2":
            xsrc = [xres16[:, dc * NOWN + tc * 512: dc * NOWN + (tc + 1) * 512]
                    for tc in range(2)]
        else:
            xsrc = [x16[:, pg * NOWN + tc * 512: pg * NOWN + (tc + 1) * 512]
                    for tc in range(2)]
        for tc in range(2):
            sch.op("t", call("matmul", ps_sum[:, tc * 512:(tc + 1) * 512], ones[:],
                             xsrc[tc], start=(dc == 0), stop=(dc == NKC - 1)),
                   waits=[vtok, stok] if tc == 0 else [])
            sch.op("t", call("matmul", ps_sq[:, tc * 512:(tc + 1) * 512], ones[:],
                             sq16[:, pg * NOWN + tc * 512: pg * NOWN + (tc + 1) * 512],
                             start=(dc == 0), stop=(dc == NKC - 1)),
                   incs=[(sT, 1)] if tc == 1 else [])
        statT[(ph, dc)] = sT.n

    for si0 in (0, 1):
        if si0 not in wtok:
            wtok[si0] = qL.dma("q", call(
                "dma_start", out=wbuf[:, si0 * D:(si0 + 1) * D],
                in_=wo_sw[si0 * 128:(si0 + 1) * 128, :]), waits=gate)
    for dc in range(NKC):
        pg = dc % 2
        if dc + 1 < NKC and (dc + 1) not in wtok:
            wtok[dc + 1] = qL.dma("q", call(
                "dma_start", out=wbuf[:, (1 - pg) * D:(2 - pg) * D],
                in_=wo_sw[(dc + 1) * 128:(dc + 2) * 128, :]), waits=[(sT, sT.n)])
        for hv in range(H):
            for tc in range(2):
                w0 = []
                if hv == 0 and tc == 0:
                    w0 = [wtok[dc], ot_own, cn]
                    lk = slot_last.pop(("psW", pg), None)
                    if lk is not None:
                        w0.extend(lk if isinstance(lk, list) else [lk])
                if hv == HPC and tc == 0:
                    w0 = [ot_mate]
                sch.op("t", call("matmul", psW[:, (2 * pg + tc) * 512:(2 * pg + tc + 1) * 512],
                                 wbuf[:, pg * D + hv * 128: pg * D + (hv + 1) * 128],
                                 OTsb[:, hv * NOWN + tc * 512: hv * NOWN + (tc + 1) * 512],
                                 start=(hv == 0), stop=(hv == H - 1)),
                       waits=w0, incs=[(sT, 1)] if hv == H - 1 and tc == 1 else [])
        sl = aP[:, (dc % 4) * NOWN:((dc % 4) + 1) * NOWN]
        ew = [sT]
        lk = slot_last.pop(("aP", dc % 4), None)
        if lk is not None:
            ew.extend(lk if isinstance(lk, list) else [lk])
        sch.op("v", call("tensor_copy", sl, psW[:, 2 * pg * 512: 2 * pg * 512 + 1024]),
               waits=ew, incs=[(sV, 1)])
        ev = (sV, sV.n)
        slot_last[("psW", pg)] = ev
        sw_ = [sV] if dc < 2 else [sV, (sT, statT[("wo", dc - 2)])]
        sch.op("s", call("activation", sq16[:, pg * NOWN:(pg + 1) * NOWN], sl,
                         AF.Square), waits=sw_, incs=[(sS, 1)])
        sqv = (sS, sS.n)
        xw16 = [] if dc < 2 else [(sT, statT[("wo", dc - 2)])]
        lkx = slot_last.pop(("x16", pg), None)
        if lkx is not None:
            xw16.append(lkx)
        sch.op("v", call("tensor_copy", x16[:, pg * NOWN:(pg + 1) * NOWN], sl),
               waits=xw16, incs=[(sV, 1)])
        xv = (sV, sV.n)
        stat_q.append(("wo", dc, pg, xv, sqv))
        if dc > 0:
            emit_stats(psSum, psSq, *stat_q.pop(0))
        sp_at[dc] = pL.dma("y", call("dma_start",
                                     out=attnP[dc * 128:(dc + 1) * 128, :],
                                     in_=x16[:, pg * NOWN:(pg + 1) * NOWN]),
                           waits=[xv])
        slot_last[("x16", pg)] = sp_at[dc]
        slot_last[("aP", dc % 4)] = [xv, sqv]
    emit_stats(psSum, psSq, *stat_q.pop(0))
    wo_last = (sT, sT.n)
    ma_tok = lnfin(m_a[:], r_a[:], psSum, psSq)
    bscope.close()
    xres16 = ctx.enter_context(nc.sbuf_tensor("xres16", [128, NKC * NOWN], BF16))
    A_sb = ctx.enter_context(nc.sbuf_tensor("A_sb", [128, NKC * 256], BF16))
    B_sb = ctx.enter_context(nc.sbuf_tensor("B_sb", [128, 2 * D], BF16))
    hs16 = ctx.enter_context(nc.sbuf_tensor("hs16", [128, 2 * NOWN], BF16))
    atok = qL.dma("q", call("dma_start", out=A_sb[:], in_=A_sw[:]), waits=[wo_last])
    btok = qL.dma("q", call("dma_start", out=B_sb[:], in_=B_sw[:]), waits=[wo_last])
    wtok[NKC] = qL.dma("q", call("dma_start", out=wbuf[:, 0:D],
                                 in_=WT_sw[0:128, :]), waits=[wo_last])
    wtok[NKC + 1] = qL.dma("q", call("dma_start", out=wbuf[:, D:2 * D],
                                     in_=WT_sw[128:256, :]), waits=[wo_last])

    resid_tok = {}
    rload = {}

    def resid_load(dc):
        s4 = dc % 4
        w = [sp_at[dc]]
        lk = slot_last.pop(("fb16", s4), None)
        if lk is not None:
            w.extend(lk if isinstance(lk, list) else [lk])
        atk = sL.dma("s", call("dma_start", out=fb16[:, s4 * NOWN:(s4 + 1) * NOWN],
                               in_=attnP[dc * 128:(dc + 1) * 128, :]), waits=w)
        xw = []
        lk = slot_last.pop(("xinb", s4), None)
        if lk is not None:
            xw.extend(lk if isinstance(lk, list) else [lk])
        xtk = yL.dma("y", call("dma_start", out=xinb[:, s4 * NOWN:(s4 + 1) * NOWN],
                               in_=xin[dc * 128:(dc + 1) * 128, :]), waits=xw)
        rload[dc] = (atk, xtk)

    def resid_chunk_ops(dc):
        s4 = dc % 4
        sl = aP[:, s4 * NOWN:(s4 + 1) * NOWN]
        atk, xtk = rload.pop(dc)
        xr = xrb[:, s4 * NOWN:(s4 + 1) * NOWN]
        lk = slot_last.pop(("xrb", s4), None)
        lk = [] if lk is None else (lk if isinstance(lk, list) else [lk])
        lka = slot_last.pop(("aP", s4), None)
        lka = [] if lka is None else (lka if isinstance(lka, list) else [lka])
        steps = []
        for eng, c0, c1 in (("v", 0, 640), ("q", 640, 1024)):
            w0 = [atk, ma_tok] + lka + ([wo_last] if dc == 0 and eng == "q" else [])
            steps.append((eng, call("tensor_tensor", sl[:, c0:c1],
                                    fb16[:, s4 * NOWN + c0: s4 * NOWN + c1],
                                    m_a[:, c0:c1], ALU.subtract), w0, []))
            steps.append((eng, call("tensor_tensor", sl[:, c0:c1], sl[:, c0:c1],
                                    r_a[:, c0:c1], ALU.mult), [], []))
            w1 = [xtk] + lk
            steps.append((eng, call("tensor_tensor", xr[:, c0:c1], sl[:, c0:c1],
                                    xinb[:, s4 * NOWN + c0: s4 * NOWN + c1],
                                    ALU.add), w1, []))
            steps.append((eng, call("tensor_copy",
                                    xres16[:, dc * NOWN + c0: dc * NOWN + c1],
                                    xr[:, c0:c1]), [], ["tok"]))
        return steps

    def emit_pairwise(step_lists, fini):
        idx = [0] * len(step_lists)
        toks = [[] for _ in step_lists]
        done = 0
        while done < len(step_lists):
            done = 0
            for ci, steps in enumerate(step_lists):
                if idx[ci] >= len(steps):
                    done += 1
                    continue
                eng, fn, ws, mark = steps[idx[ci]]
                sem = sV if eng == "v" else sQ
                sch.op(eng, fn, waits=ws,
                       incs=[(sem, 1)] if mark else [])
                if mark:
                    toks[ci].append((sem, sem.n))
                idx[ci] += 1
        for ci, f in enumerate(fini):
            f(toks[ci])

    def resid_fini(dc):
        s4 = dc % 4
        xr = xrb[:, s4 * NOWN:(s4 + 1) * NOWN]

        def f(toks):
            resid_tok[dc] = toks
            sp_xr[dc] = E.vL.dma("q", call(
                "dma_start", out=xres[dc * 128:(dc + 1) * 128, :], in_=xr),
                waits=toks)
            slot_last[("xrb", s4)] = sp_xr[dc]
            slot_last[("xinb", s4)] = list(toks)
            slot_last[("fb16", s4)] = list(toks)
            slot_last[("aP", s4)] = list(toks)
            sw_ = list(toks) + ([] if dc < 2 else [(sT, statT[("l2", dc - 2)])])
            sch.op("s", call("activation",
                             sq16[:, (dc % 2) * NOWN:((dc % 2) + 1) * NOWN],
                             xr, AF.Square), waits=sw_, incs=[(sS, 1)])
            stat_q.append(("l2", dc, dc % 2, toks[-1], (sS, sS.n)))
        return f

    def l2_stats(d2):
        ph, _, pg2, vt, st_ = stat_q.pop(0)
        rt = resid_tok[d2]
        sch.op("t", call("matmul", psSum[:, 0:512], ones[:],
                         xres16[:, d2 * NOWN: d2 * NOWN + 512],
                         start=(d2 == 0), stop=(d2 == NKC - 1)),
               waits=[rt[0], rt[1], st_])
        sch.op("t", call("matmul", psSum[:, 512:1024], ones[:],
                         xres16[:, d2 * NOWN + 512: d2 * NOWN + 1024],
                         start=(d2 == 0), stop=(d2 == NKC - 1)))
        sch.op("t", call("matmul", psSq[:, 0:512], ones[:],
                         sq16[:, pg2 * NOWN: pg2 * NOWN + 512],
                         start=(d2 == 0), stop=(d2 == NKC - 1)))
        sch.op("t", call("matmul", psSq[:, 512:1024], ones[:],
                         sq16[:, pg2 * NOWN + 512: pg2 * NOWN + 1024],
                         start=(d2 == 0), stop=(d2 == NKC - 1)),
               incs=[(sT, 1)])
        statT[("l2", d2)] = sT.n

    for dpre in range(3):
        resid_load(dpre)
    for dc in range(0, NKC, 2):
        while stat_q and stat_q[0][1] <= dc - 1:
            l2_stats(stat_q[0][1])
        emit_pairwise([resid_chunk_ops(dc), resid_chunk_ops(dc + 1)],
                      [resid_fini(dc), resid_fini(dc + 1)])
        for dn in (dc + 3, dc + 4):
            if dn < NKC:
                resid_load(dn)
    while stat_q:
        l2_stats(stat_q[0][1])
    m2_tok = lnfin(m_2[:], r_2[:], psSum, psSq)
    sch.op("v", call("tensor_copy", mr16[:, 0:NOWN], m_2[:]), waits=[m2_tok])
    sch.op("v", call("tensor_copy", mr16[:, NOWN:2 * NOWN], r_2[:]), incs=[(sV, 1)])
    mr_tok = (sV, sV.n)

    ln2t_tok = {}
    for dc in range(NKC):
        toks = []
        for eng, c0, c1 in (("v", 0, 640), ("q", 640, 1024)):
            xs = xres16[:, dc * NOWN + c0: dc * NOWN + c1]
            sch.op(eng, call("tensor_tensor", xs, xs, mr16[:, c0:c1], ALU.subtract),
                   waits=[mr_tok, (sT, statT[("l2", NKC - 1)])])
            sch.op(eng, call("tensor_tensor", xs, xs, mr16[:, NOWN + c0: NOWN + c1],
                             ALU.mult),
                   incs=[(sV, 1)] if eng == "v" else [(sQ, 1)])
            toks.append((sV, sV.n) if eng == "v" else (sQ, sQ.n))
        ln2t_tok[dc] = toks
    LN2T = xres16
    for dc in range(NKC):
        for erc in range(2):
            for tc in range(2):
                w0 = []
                if tc == 0 and erc == 0:
                    w0 = list(ln2t_tok[dc]) + ([atok, btok, cn] if dc == 0 else [])
                    if dc == 0:
                        for kk in (("psW", 0), ("psW", 1)):
                            lk = slot_last.pop(kk, None)
                            if lk is not None:
                                w0.extend(lk if isinstance(lk, list) else [lk])
                sch.op("t", call("matmul",
                                 psW[:, (2 * erc + tc) * 512:(2 * erc + tc + 1) * 512],
                                 A_sb[:, dc * 256 + erc * 128: dc * 256 + (erc + 1) * 128],
                                 LN2T[:, dc * NOWN + tc * 512: dc * NOWN + (tc + 1) * 512],
                                 start=(dc == 0), stop=(dc == NKC - 1)),
                       waits=w0,
                       incs=[(sT, 1)] if dc == NKC - 1 and erc == 1 and tc == 1 else [])
    for erc in range(2):
        sch.op("v", call("tensor_scalar", hs16[:, erc * NOWN:(erc + 1) * NOWN],
                         psW[:, 2 * erc * 512: 2 * erc * 512 + 1024],
                         probs_sb[:, erc:erc + 1], None, ALU.mult),
               waits=[sT], incs=[(sV, 1)])
    hs_done = (sV, sV.n)
    slot_last[("psW", 0)] = hs_done
    slot_last[("psW", 1)] = hs_done

    for oc in range(NKC):
        pg = oc % 2
        sio = NKC + oc
        if oc + 1 < NKC and (sio + 1) not in wtok:
            wtok[sio + 1] = qL.dma("q", call(
                "dma_start", out=wbuf[:, (1 - pg) * D:(2 - pg) * D],
                in_=WT_sw[(oc + 1) * 128:(oc + 2) * 128, :]), waits=[(sT, sT.n)])
        for dc in range(NKC):
            for tc in range(2):
                w0 = []
                if dc == 0 and tc == 0:
                    w0 = [wtok[sio], hs_done]
                    lk = slot_last.pop(("psW", pg), None)
                    if lk is not None:
                        w0.extend(lk if isinstance(lk, list) else [lk])
                sch.op("t", call("matmul",
                                 psW[:, (2 * pg + tc) * 512:(2 * pg + tc + 1) * 512],
                                 wbuf[:, pg * D + dc * 128: pg * D + (dc + 1) * 128],
                                 LN2T[:, dc * NOWN + tc * 512: dc * NOWN + (tc + 1) * 512],
                                 start=(dc == 0), stop=False), waits=w0)
        for erc in range(2):
            for tc in range(2):
                sch.op("t", call("matmul",
                                 psW[:, (2 * pg + tc) * 512:(2 * pg + tc + 1) * 512],
                                 B_sb[:, erc * D + oc * 128: erc * D + (oc + 1) * 128],
                                 hs16[:, erc * NOWN + tc * 512: erc * NOWN + (tc + 1) * 512],
                                 start=False, stop=(erc == 1)),
                       incs=[(sT, 1)] if erc == 1 and tc == 1 else [])
        s4 = oc % 4
        sl = aP[:, s4 * NOWN:(s4 + 1) * NOWN]
        w = [sT]
        lk = slot_last.pop(("aP", s4), None)
        if lk is not None:
            w.extend(lk if isinstance(lk, list) else [lk])
        if (oc - 2) in sp_ff:
            w.append(sp_ff[oc - 2])
        sch.op("v", call("tensor_scalar", sl, psW[:, 2 * pg * 512: 2 * pg * 512 + 1024],
                         bias_sb[:, oc:oc + 1], None, ALU.add),
               waits=w, incs=[(sV, 1)])
        ev = (sV, sV.n)
        slot_last[("psW", pg)] = ev
        sw_ = [sV] if oc < 2 else [sV, (sT, statT[("l3", oc - 2)])]
        sch.op("s", call("activation", sq16[:, pg * NOWN:(pg + 1) * NOWN], sl,
                         AF.Square), waits=sw_, incs=[(sS, 1)])
        sqv = (sS, sS.n)
        xw16 = [] if oc < 2 else [(sT, statT[("l3", oc - 2)])]
        lkx = slot_last.pop(("x16", pg), None)
        if lkx is not None:
            xw16.append(lkx)
        sch.op("v", call("tensor_copy", x16[:, pg * NOWN:(pg + 1) * NOWN], sl),
               waits=xw16, incs=[(sV, 1)])
        xv = (sV, sV.n)
        stat_q.append(("l3", oc, pg, xv, sqv))
        if oc > 0:
            emit_stats(psSum, psSq, *stat_q.pop(0))
        sp_ff[oc] = pL.dma("y", call("dma_start",
                                     out=ffnX[oc * 128:(oc + 1) * 128, :],
                                     in_=x16[:, pg * NOWN:(pg + 1) * NOWN]),
                           waits=[xv])
        slot_last[("x16", pg)] = sp_ff[oc]
        slot_last[("aP", s4)] = [xv, sqv]
    emit_stats(psSum, psSq, *stat_q.pop(0))
    m3_tok = lnfin(m_3[:], r_3[:], psSum, psSq)
    xw0 = [m3_tok]
    for lkx in (slot_last.pop(("x16", 0), None), slot_last.pop(("x16", 1), None)):
        if lkx is not None:
            xw0.append(lkx)
    sch.op("v", call("tensor_copy", x16[:, 0:NOWN], m_3[:]), waits=xw0)
    sch.op("v", call("tensor_copy", x16[:, NOWN:2 * NOWN], r_3[:]), incs=[(sV, 1)])
    m316_tok = (sV, sV.n)

    fload = {}

    def fin_load(dc):
        s4 = dc % 4
        w = [sp_ff[dc]]
        lk = slot_last.pop(("fb16", s4), None)
        if lk is not None:
            w.extend(lk if isinstance(lk, list) else [lk])
        ftk = E.vL.dma("q", call("dma_start", out=fb16[:, s4 * NOWN:(s4 + 1) * NOWN],
                                 in_=ffnX[dc * 128:(dc + 1) * 128, :]), waits=w)
        xr = xrb[:, s4 * NOWN:(s4 + 1) * NOWN]
        xw = [sp_xr[dc]]
        lk = slot_last.pop(("xrb", s4), None)
        if lk is not None:
            xw.extend(lk if isinstance(lk, list) else [lk])
        rtk = sL.dma("s", call("dma_start", out=xr,
                               in_=xres[dc * 128:(dc + 1) * 128, :]), waits=xw)
        fload[dc] = (ftk, rtk)

    def fin_chunk_ops(dc):
        s4 = dc % 4
        s2 = dc % 2
        t16 = sq16[:, s2 * NOWN:(s2 + 1) * NOWN]
        xr = xrb[:, s4 * NOWN:(s4 + 1) * NOWN]
        ftk, rtk = fload.pop(dc)
        lks = slot_last.pop(("sq16", s2), None)
        lks = [] if lks is None else (lks if isinstance(lks, list) else [lks])
        steps = []
        for eng, c0, c1 in (("v", 0, 640), ("q", 640, 1024)):
            steps.append((eng, call("tensor_tensor", t16[:, c0:c1],
                                    fb16[:, s4 * NOWN + c0: s4 * NOWN + c1],
                                    x16[:, c0:c1], ALU.subtract),
                          [ftk, m316_tok] + lks, []))
            steps.append((eng, call("tensor_tensor", t16[:, c0:c1], t16[:, c0:c1],
                                    x16[:, NOWN + c0: NOWN + c1], ALU.mult),
                          [], []))
            steps.append((eng, call("tensor_tensor", xr[:, c0:c1], t16[:, c0:c1],
                                    xr[:, c0:c1], ALU.add), [rtk], ["tok"]))
        return steps

    def fin_fini(dc):
        s4 = dc % 4
        xr = xrb[:, s4 * NOWN:(s4 + 1) * NOWN]

        def f(toks):
            fin_done[dc] = toks
            tok = oF.dma("y", call("dma_start",
                                   out=outT[dc * 128:(dc + 1) * 128, :], in_=xr),
                         waits=toks)
            slot_last[("xrb", s4)] = tok
            slot_last[("fb16", s4)] = list(toks)
            slot_last[("sq16", dc % 2)] = list(toks)
        return f

    for dpre in range(3):
        fin_load(dpre)
    for dc in range(0, NKC, 2):
        emit_pairwise([fin_chunk_ops(dc), fin_chunk_ops(dc + 1)],
                      [fin_fini(dc), fin_fini(dc + 1)])
        for dn in (dc + 3, dc + 4):
            if dn < NKC:
                fin_load(dn)
    sch.op("y", call("nop"), waits=oF.totals())


def _mk_env(nc, sch, ctx):
    E = SimpleNamespace(nc=nc, sch=sch, ctx=ctx)
    E.dC = sch.sem("dC")
    E.yL = Lanes(sch, "yL", 4)
    E.qL = Lanes(sch, "qL", 2)
    E.oL = Lanes(sch, "oL", 2)
    E.pL = Lanes(sch, "pL", 4)
    E.sL = Lanes(sch, "sL", 4)
    E.vL = Lanes(sch, "vL", 2)
    E.oF = Lanes(sch, "oF", 2)
    E.sT = sch.sem("sT")
    E.sV = sch.sem("sV")
    E.sS = sch.sem("sS")
    E.sQ = sch.sem("sQ")
    sch.chain = {"v": E.sV, "s": E.sS, "q": E.sQ}
    return E


def _load_consts(E, srcs):
    nc, sch = E.nc, E.sch
    E.ident = E.ctx.enter_context(nc.sbuf_tensor("ident", [128, 128], BF16))
    E.ones = E.ctx.enter_context(nc.sbuf_tensor("ones", [128, 128], BF16))
    E.onesrow = E.ctx.enter_context(nc.sbuf_tensor("onesrow", [1, 128], BF16))
    E.negrow = E.ctx.enter_context(nc.sbuf_tensor("negrow", [1, 128], BF16))
    E.mask = E.ctx.enter_context(nc.sbuf_tensor("mask", [128, 128], BF16))
    E.maskT = E.ctx.enter_context(nc.sbuf_tensor("maskT", [128, 128], BF16))
    for dst, src in ((E.ident, srcs["identA"]), (E.ones, srcs["onesA"]),
                     (E.onesrow, srcs["onesrowA"]), (E.negrow, srcs["negrowA"]),
                     (E.mask, srcs["maskA"]), (E.maskT, srcs["maskTA"])):
        sch.op("y", call("dma_start", out=dst[:], in_=src[:]), incs=[(E.dC, 16)])
    E.consts = (E.dC, E.dC.n)


def _declare_const_drams(nc):
    return {
        "identA": nc.dram_tensor("identA", [128, 128], BF16, kind="ExternalInput").ap(),
        "onesA": nc.dram_tensor("onesA", [128, 128], BF16, kind="ExternalInput").ap(),
        "onesrowA": nc.dram_tensor("onesrowA", [1, 128], BF16, kind="ExternalInput").ap(),
        "negrowA": nc.dram_tensor("negrowA", [1, 128], BF16, kind="ExternalInput").ap(),
        "maskA": nc.dram_tensor("maskA", [128, 128], BF16, kind="ExternalInput").ap(),
        "maskTA": nc.dram_tensor("maskTA", [128, 128], BF16, kind="ExternalInput").ap(),
    }


def _declare_bc_drams(nc, internal_otf):
    d = {}
    if not internal_otf:
        d["OTf"] = nc.dram_tensor("OTf", [H * 128, NOWN], BF16, kind="ExternalInput").ap()
    d["xin"] = nc.dram_tensor("xin", [D, NOWN], F32, kind="ExternalInput").ap()
    d["wo_sw"] = nc.dram_tensor("wo_sw", [NKC * 128, D], F32, kind="ExternalInput").ap()
    d["WT_sw"] = nc.dram_tensor("WT_sw", [NKC * 128, D], F32, kind="ExternalInput").ap()
    d["A_sw"] = nc.dram_tensor("A_sw", [128, NKC * 256], F32, kind="ExternalInput").ap()
    d["B_sw"] = nc.dram_tensor("B_sw", [128, 2 * D], F32, kind="ExternalInput").ap()
    d["bias_sw"] = nc.dram_tensor("bias_sw", [128, NKC], F32, kind="ExternalInput").ap()
    d["probs_sw"] = nc.dram_tensor("probs_sw", [128, 2], F32, kind="ExternalInput").ap()
    d["outT"] = nc.dram_tensor("outT", [D, NOWN], F32, kind="ExternalOutput").ap()
    d["attnP"] = nc.dram_tensor("attnP", [D, NOWN], BF16, kind="Internal").ap()
    d["xres"] = nc.dram_tensor("xres", [D, NOWN], F32, kind="Internal").ap()
    d["ffnX"] = nc.dram_tensor("ffnX", [D, NOWN], BF16, kind="Internal").ap()
    return d


def build_fused():
    nc = bass.Bass("TRN2", target_bir_lowering=False)
    hsT = nc.dram_tensor("hsT", [D, S], F32, kind="ExternalInput").ap()
    wqkv = nc.dram_tensor("wqkv", [HPC * 128, NKC * 384], F32, kind="ExternalInput").ap()
    csrc = _declare_const_drams(nc)
    CCsh = nc.dram_tensor("CCsh", [H * 128, S], BF16, kind="Internal",
                          addr_space="Shared").ap()
    bbin = nc.dram_tensor("bbin", [1, 16], F32, kind="Internal").ap()
    bbout = nc.dram_tensor("bbout", [8, 16], F32, kind="Internal",
                           addr_space="Shared").ap()
    bc = _declare_bc_drams(nc, internal_otf=True)

    sch = Sched()
    ctx = ExitStack()
    with ctx:
        E = _mk_env(nc, sch, ctx)
        cc = sch.sem("cc")
        _load_consts(E, csrc)

        base_cell = {}

        def getpid(e):
            if "pid" not in base_cell:
                base_cell["pid"] = e.partition_id()
            return base_cell["pid"]

        def ot_store(h, src, waits):
            def fn(e, h=h, src=src):
                if "v" not in base_cell:
                    base_cell["v"] = (getpid(e) % 2) * (HPC * 128)
                dst = CCsh[bass.DynSlice(base_cell["v"], HPC * 128), :]
                return e.dma_start(out=dst[h * 128:(h + 1) * 128, :], in_=src)
            return E.oL.dma("q", fn, waits=waits)
        E.ot_store = ot_store
        E.hsT, E.wqkv = hsT, wqkv
        E.pre_wtok = {}

        def proj_hook(hh, p, tok):
            if hh == HPC - 1 and p == 5:
                E.pre_wtok[0] = E.qL.dma("q", lambda e: e.dma_start(
                    out=E.wbuf[:, 0:D], in_=bc["wo_sw"][0:128, :]), waits=[tok])
                E.pre_wtok[1] = E.qL.dma("q", lambda e: e.dma_start(
                    out=E.wbuf[:, D:2 * D], in_=bc["wo_sw"][128:256, :]), waits=[tok])
        E.proj_hook = proj_hook
        ascope = ExitStack()
        _phaseA(E, ascope)
        ascope.close()
        wbuf = ctx.enter_context(nc.sbuf_tensor("wbuf", [128, 2 * D], BF16))
        E.wbuf = wbuf
        btok = E.qL.dma("q", call("dma_start", out=bbin[:], in_=E.ones[0:1, 0:16]))
        sch.op("q", call("collective_compute", "AllGather", ALU.bypass,
                         [list(range(8))], [bbin.opt()], [bbout.opt()]),
               waits=E.oL.totals() + [btok], incs=[(cc, 1)], chainable=False)

        def otf_own(e):
            pid = getpid(e)
            return CCsh[bass.DynSlice((pid % 2) * (HPC * 128), HPC * 128),
                        bass.DynSlice((pid % 2) * NOWN, NOWN)].rearrange(
                "(hv p) t -> p hv t", p=128)

        def otf_mate(e):
            pid = getpid(e)
            return CCsh[bass.DynSlice(((pid + 1) % 2) * (HPC * 128), HPC * 128),
                        bass.DynSlice((pid % 2) * NOWN, NOWN)].rearrange(
                "(hv p) t -> p hv t", p=128)
        E.otf_own, E.otf_mate = otf_own, otf_mate
        E.otf_eng = "q"
        E.own_gate = E.oL.totals()
        E.gate = [(cc, 1)]
        for k in ("xin", "wo_sw", "WT_sw", "A_sw", "B_sw", "bias_sw", "probs_sw",
                  "outT", "attnP", "xres", "ffnX"):
            setattr(E, k, bc[k])
        _phaseBC(E)
        sch.emit(nc, ctx)
    return nc


def build_progA():
    nc = bass.Bass("TRN2", target_bir_lowering=False)
    hsT = nc.dram_tensor("hsT", [D, S], F32, kind="ExternalInput").ap()
    wqkv = nc.dram_tensor("wqkv", [HPC * 128, NKC * 384], F32, kind="ExternalInput").ap()
    csrc = _declare_const_drams(nc)
    OT = nc.dram_tensor("OT", [HPC * 128, S], BF16, kind="ExternalOutput").ap()
    sch = Sched()
    ctx = ExitStack()
    with ctx:
        E = _mk_env(nc, sch, ctx)
        _load_consts(E, csrc)

        def ot_store(h, src, waits):
            return E.oL.dma("y", call("dma_start", out=OT[h * 128:(h + 1) * 128, :],
                                      in_=src), waits=waits)
        E.ot_store = ot_store
        E.hsT, E.wqkv = hsT, wqkv
        ascope = ExitStack()
        _phaseA(E, ascope)
        sch.op("y", call("nop"), waits=E.oL.totals())
        ascope.close()
        sch.emit(nc, ctx)
    return nc


def build_progB():
    nc = bass.Bass("TRN2", target_bir_lowering=False)
    csrc = _declare_const_drams(nc)
    bc = _declare_bc_drams(nc, internal_otf=False)
    sch = Sched()
    ctx = ExitStack()
    with ctx:
        E = _mk_env(nc, sch, ctx)
        _load_consts(E, csrc)
        OTf = bc["OTf"]
        E.otf_own = lambda e: OTf[0:HPC * 128, :].rearrange("(hv p) t -> p hv t", p=128)
        E.otf_mate = lambda e: OTf[HPC * 128:, :].rearrange("(hv p) t -> p hv t", p=128)
        E.own_gate = []
        E.otf_eng = "y"
        E.gate = []
        for k in ("xin", "wo_sw", "WT_sw", "A_sw", "B_sw", "bias_sw", "probs_sw",
                  "outT", "attnP", "xres", "ffnX"):
            setattr(E, k, bc[k])
        _phaseBC(E)
        sch.emit(nc, ctx)
    return nc


def _sw(arr2d):
    K, M = arr2d.shape
    return np.ascontiguousarray(
        arr2d.reshape(K // 128, 128, M).transpose(1, 0, 2).reshape(128, -1))


_prog = None


def kernel(L0_attn_hidden_states, L0_attn_wq, L0_attn_wkv, L0_attn_wo,
           L0_ffn_W, L0_ffn_b, L0_ffn_A, L0_ffn_B, L0_ffn_probs):
    global _prog
    hs = np.ascontiguousarray(L0_attn_hidden_states, dtype=np.float32)
    wq = np.asarray(L0_attn_wq, dtype=np.float32)
    wkv = np.asarray(L0_attn_wkv, dtype=np.float32)
    wo = np.asarray(L0_attn_wo, dtype=np.float32)
    W = np.asarray(L0_ffn_W, dtype=np.float32)
    bia = np.asarray(L0_ffn_b, dtype=np.float32)
    A = np.asarray(L0_ffn_A, dtype=np.float32)
    Bw = np.asarray(L0_ffn_B, dtype=np.float32)
    probs = np.asarray(L0_ffn_probs, dtype=np.float32)

    ident = np.eye(128, dtype=BF)
    ones128 = np.ones((128, 128), dtype=BF)
    onesrow = np.ones((1, 128), dtype=BF)
    negrow = np.full((1, 128), -1.0, dtype=BF)
    ii, jj = np.indices((128, 128))
    mask = np.where(jj <= ii, 0.0, NEG).astype(BF)
    maskT = np.ascontiguousarray(mask.T)

    hsT = [np.ascontiguousarray(hs[b].T) for b in range(B)]
    wqkv_p = []
    for p in range(2):
        blocks = []
        for hl in range(HPC):
            g = p * HPC + hl
            blk = np.concatenate([
                _sw(wq[:, g * DQ: g * DQ + DN]),
                _sw(wkv[:, g * DN: (g + 1) * DN]),
                _sw(wkv[:, H * DN + g * DV: H * DN + (g + 1) * DV])], axis=1)
            blocks.append(blk)
        wqkv_p.append(np.ascontiguousarray(np.concatenate(blocks, axis=0)))

    wo_sw_p = []
    for p in range(2):
        order = list(range(p * HPC, (p + 1) * HPC)) + \
            list(range((1 - p) * HPC, (2 - p) * HPC))
        wo_perm = np.concatenate([wo[g * DV:(g + 1) * DV, :] for g in order], axis=0)
        wo_sw_p.append(np.ascontiguousarray(np.concatenate(
            [_sw(wo_perm[:, dc * 128:(dc + 1) * 128]) for dc in range(NKC)], axis=0)))
    WT_sw = np.ascontiguousarray(np.concatenate(
        [_sw(np.ascontiguousarray(W[oc * 128:(oc + 1) * 128, :].T)) for oc in range(NKC)], axis=0))
    A_sw = _sw(np.ascontiguousarray(A.transpose(2, 0, 1).reshape(D, 256)))
    B_sw = np.ascontiguousarray(
        Bw.transpose(0, 2, 1).reshape(256, D).reshape(2, 128, D)
        .transpose(1, 0, 2).reshape(128, 2 * D))
    bias_sw = np.ascontiguousarray(bia.reshape(NKC, 128).T)
    probs_sw = np.ascontiguousarray(np.repeat(probs, 16).reshape(2, 128).T)

    if _prog is None:
        _prog = build_fused()
    in_maps = []
    for c in range(8):
        b, p = c // 2, c % 2
        own = slice(p * NOWN, (p + 1) * NOWN)
        in_maps.append(dict(
            hsT=hsT[b], wqkv=wqkv_p[p],
            identA=ident, onesA=ones128, onesrowA=onesrow, negrowA=negrow,
            maskA=mask, maskTA=maskT,
            xin=np.ascontiguousarray(hsT[b][:, own]), wo_sw=wo_sw_p[p], WT_sw=WT_sw,
            A_sw=A_sw, B_sw=B_sw, bias_sw=bias_sw, probs_sw=probs_sw))
    res = run_bass_kernel_spmd(_prog, in_maps, core_ids=list(range(8)))

    out = np.empty((B, S, D), dtype=np.float32)
    for c in range(8):
        b, p = c // 2, c % 2
        out[b, p * NOWN:(p + 1) * NOWN, :] = res.results[c]["outT"].T
    return out
